# version 03 (final)
# baseline (speedup 1.0000x reference)
import math
import numpy as np
import concourse.bass as bass
import concourse.mybir as mybir
from concourse.bass_utils import run_bass_kernel_spmd

AF = mybir.ActivationFunctionType
ALU = mybir.AluOpType
AX = mybir.AxisListType
F32 = mybir.dt.float32
BF16 = mybir.dt.bfloat16
I32 = mybir.dt.int32

ENGS = ("sync", "tensor", "vector", "scalar", "gpsimd")
T = 2048
D = 1024
FD = 8
NC = 8


def bc(ap2d, n):
    return bass.AP(ap2d.tensor, ap2d.offset, list(ap2d.ap) + [[0, n]])


def A(x):
    return x if isinstance(x, bass.AP) else x[:]


class Prog:
    def __init__(self):
        self.nc = bass.Bass("TRN2", target_bir_lowering=False)
        self._ctx = []
        self.order = []
        self.cum = {e: 0 for e in ENGS}

    def _enter(self, cm):
        v = cm.__enter__()
        self._ctx.append(cm)
        return v

    def sbuf(self, name, shape, dt):
        return self._enter(self.nc.sbuf_tensor(name, shape, dt))

    def psum(self, name, shape, dt=F32):
        return self._enter(self.nc.psum_tensor(name, shape, dt))

    def hbm_in(self, name, shape, dt=F32):
        return self.nc.dram_tensor(name, list(shape), dt, kind="ExternalInput").ap()

    def hbm_out(self, name, shape, dt=F32):
        return self.nc.dram_tensor(name, list(shape), dt, kind="ExternalOutput").ap()

    def op(self, engine, inc, fn):
        self.order.append((engine, fn, inc, dict(self.cum)))
        self.cum[engine] += inc

    def dma(self, out, in_, **kw):
        self.op("sync", 16, lambda e: e.dma_start(out=out, in_=in_, **kw))

    def mm(self, out, lhsT, rhs, start, stop):
        out, lhsT, rhs = A(out), A(lhsT), A(rhs)
        self.op("tensor", 1, lambda e: e.matmul(out, lhsT, rhs, start=start,
                                                stop=stop, skip_group_check=True))

    def trans(self, out, in_, ident):
        out, in_, ident = A(out), A(in_), A(ident)
        self.op("tensor", 1, lambda e: e.transpose(out, in_, ident))

    def act(self, out, in_, func, bias=0.0, scale=1.0):
        out, in_ = A(out), A(in_)
        bias = A(bias) if not isinstance(bias, float) else bias
        scale = A(scale) if not isinstance(scale, float) else scale
        self.op("scalar", 1, lambda e: e.activation(out, in_, func, bias=bias, scale=scale))

    def v(self, meth, *a, **kw):
        self.op("vector", 1, lambda e: getattr(e, meth)(*a, **kw))

    def g(self, meth, *a, **kw):
        self.op("gpsimd", 1, lambda e: getattr(e, meth)(*a, **kw))

    def build(self):
        nc = self.nc
        sems = {}
        cms = []
        for e in ENGS:
            cm = nc.semaphore("sem_" + e)
            sems[e] = cm.__enter__()
            cms.append(cm)
        final = dict(self.cum)
        with nc.Block() as blk:
            for eng_name in ENGS:
                mine = [(fn, inc, snap) for (e, fn, inc, snap) in self.order if e == eng_name]

                def make(eng_name=eng_name, mine=mine):
                    def body(eng):
                        last = {e: 0 for e in ENGS}
                        for (fn, inc, snap) in mine:
                            for e2 in ENGS:
                                if e2 == eng_name and eng_name == "tensor":
                                    continue
                                want = snap[e2]
                                if want > last[e2]:
                                    eng.wait_ge(sems[e2], want)
                                    last[e2] = want
                            fn(eng).then_inc(sems[eng_name], inc)
                        for e2 in ENGS:
                            if final[e2] > last[e2] and e2 != eng_name:
                                eng.wait_ge(sems[e2], final[e2])
                        if eng_name != "tensor" and final[eng_name] > 0:
                            eng.wait_ge(sems[eng_name], final[eng_name])
                    return body

                getattr(blk, eng_name)(make())
        for cm in reversed(cms):
            cm.__exit__(None, None, None)
        for cm in reversed(self._ctx):
            cm.__exit__(None, None, None)
        return nc


def setup_common(p):
    c = {}
    c["ones_col"] = p.sbuf("c_ones_col", [128, 1], F32)
    c["ones_row"] = p.sbuf("c_ones_row", [1, 128], F32)
    c["ones_row16"] = p.sbuf("c_ones_row16", [1, 128], BF16)
    c["ones_col16"] = p.sbuf("c_ones_col16", [128, 1], BF16)
    c["id32"] = p.sbuf("c_id32", [128, 128], F32)
    c["id16"] = p.sbuf("c_id16", [128, 128], BF16)
    c["pcol"] = p.sbuf("c_pcol", [128, 1], F32)
    c["frow"] = p.sbuf("c_frow", [1, 512], F32)
    idx = p.sbuf("c_idx", [128, 512], I32)
    p.g("memset", c["ones_col"][:], 1.0)
    p.g("memset", c["ones_row"][:], 1.0)
    p.g("memset", c["ones_row16"][:], 1.0)
    p.g("memset", c["ones_col16"][:], 1.0)
    p.g("iota", idx[:, 0:128], [[-1, 128]], channel_multiplier=1)
    p.v("tensor_scalar", c["id32"][:], idx[:, 0:128], 0, None, ALU.is_equal)
    p.v("tensor_copy", c["id16"][:], c["id32"][:])
    p.g("iota", idx[:, 0:1], [[0, 1]], channel_multiplier=1)
    p.v("tensor_copy", c["pcol"][:], idx[:, 0:1])
    p.g("iota", idx[0:1, 0:512], [[1, 512]], channel_multiplier=0)
    p.v("tensor_copy", c["frow"][:], idx[0:1, 0:512])
    c["idx"] = idx
    return c


def load_fm(p, sb32, hbm, K, TT, sb16=None, scale=None):
    for c in range(K // 128):
        p.dma(sb32[:, c * TT:(c + 1) * TT], hbm[c * 128:(c + 1) * 128, :])
        if sb16 is not None:
            if scale is None:
                p.v("tensor_copy", sb16[:, c * TT:(c + 1) * TT], sb32[:, c * TT:(c + 1) * TT])
            else:
                p.act(sb16[:, c * TT:(c + 1) * TT], sb32[:, c * TT:(c + 1) * TT], AF.Copy, scale=scale)


def load_w16(p, dst16, hbm, K, N, stage):
    for kc in range(K // 128):
        p.dma(stage[:, 0:N], hbm[kc * 128:(kc + 1) * 128, :])
        p.v("tensor_copy", dst16[:, kc * N:(kc + 1) * N], stage[:, 0:N])


def col_tiles(p, name, hbm_rc, stage, ps_t, com):
    n = hbm_rc.shape[0]
    out = p.sbuf(name, [128, max(n, 1)], F32)
    p.dma(stage[0:n, 0:128], hbm_rc[:, :])
    p.trans(ps_t[0:128, 0:n], stage[0:n, 0:128], com["id32"][0:n, 0:n])
    p.v("tensor_copy", out[:, 0:n], ps_t[0:128, 0:n])
    return out


def emit_ln(p, com, pre32, out32, TT, psA, psB, ps_s1, ps_s2, sq, row1, row2, out16=None):
    for t0 in range(0, TT, 512):
        for c in range(FD):
            sl = slice(c * TT + t0, c * TT + t0 + 512)
            p.mm(ps_s1[0:1, :], com["ones_col"], pre32[:, sl], start=(c == 0), stop=(c == FD - 1))
        for c in range(FD):
            sl = slice(c * TT + t0, c * TT + t0 + 512)
            p.act(sq[:, 0:512], pre32[:, sl], AF.Square)
            p.mm(ps_s2[0:1, :], com["ones_col"], sq[:, 0:512], start=(c == 0), stop=(c == FD - 1))
        p.v("tensor_scalar_mul", row1[0:1, :], ps_s1[0:1, :], 1.0 / D)
        p.v("tensor_mul", row2[0:1, :], row1[0:1, :], row1[0:1, :])
        p.v("scalar_tensor_tensor", row2[0:1, :], ps_s2[0:1, :], 1.0 / D, row2[0:1, :],
            ALU.mult, ALU.subtract)
        p.v("tensor_scalar_add", row2[0:1, :], row2[0:1, :], 1e-5)
        p.act(row2[0:1, :], row2[0:1, :], AF.Sqrt)
        p.v("reciprocal", row2[0:1, :], row2[0:1, :])
        p.mm(psA[:, 0:512], com["ones_row"], row1[0:1, :], start=True, stop=True)
        p.mm(psB[:, 0:512], com["ones_row"], row2[0:1, :], start=True, stop=True)
        for c in range(FD):
            sl = slice(c * TT + t0, c * TT + t0 + 512)
            p.v("tensor_sub", out32[:, sl], pre32[:, sl], psA[:, 0:512])
            p.v("tensor_mul", out32[:, sl], out32[:, sl], psB[:, 0:512])
            if out16 is not None:
                p.v("tensor_copy", out16[:, sl], out32[:, sl])




def emit_ln_dma(p, com, pre32, o_hbm, TT, psA, psB, ps_s1, ps_s2, sq, row1, row2,
                ostage, out16=None):
    for t0 in range(0, TT, 512):
        for c in range(FD):
            sl = slice(c * TT + t0, c * TT + t0 + 512)
            p.mm(ps_s1[0:1, :], com["ones_col"], pre32[:, sl], start=(c == 0), stop=(c == FD - 1))
        for c in range(FD):
            sl = slice(c * TT + t0, c * TT + t0 + 512)
            p.act(sq[:, 0:512], pre32[:, sl], AF.Square)
            p.mm(ps_s2[0:1, :], com["ones_col"], sq[:, 0:512], start=(c == 0), stop=(c == FD - 1))
        p.v("tensor_scalar_mul", row1[0:1, :], ps_s1[0:1, :], 1.0 / D)
        p.v("tensor_mul", row2[0:1, :], row1[0:1, :], row1[0:1, :])
        p.v("scalar_tensor_tensor", row2[0:1, :], ps_s2[0:1, :], 1.0 / D, row2[0:1, :],
            ALU.mult, ALU.subtract)
        p.v("tensor_scalar_add", row2[0:1, :], row2[0:1, :], 1e-5)
        p.act(row2[0:1, :], row2[0:1, :], AF.Sqrt)
        p.v("reciprocal", row2[0:1, :], row2[0:1, :])
        p.mm(psA[:, 0:512], com["ones_row"], row1[0:1, :], start=True, stop=True)
        p.mm(psB[:, 0:512], com["ones_row"], row2[0:1, :], start=True, stop=True)
        for c in range(FD):
            sl = slice(c * TT + t0, c * TT + t0 + 512)
            p.v("tensor_sub", ostage[:, 0:512], pre32[:, sl], psA[:, 0:512])
            p.v("tensor_mul", ostage[:, 0:512], ostage[:, 0:512], psB[:, 0:512])
            if out16 is not None:
                p.v("tensor_copy", out16[:, sl], ostage[:, 0:512])
            p.dma(o_hbm[c * 128:(c + 1) * 128, t0:t0 + 512], ostage[:, 0:512])


def build_proj(K, N, bias=True):
    p = Prog()
    com = setup_common(p)
    x_h = p.hbm_in("x", [K, T])
    w_h = p.hbm_in("w", [K, N])
    o_h = p.hbm_out("o", [N, T])
    stage = p.sbuf("stage", [128, max(N, T)], F32)
    x32 = p.sbuf("x32", [128, (K // 128) * T], F32)
    x16 = p.sbuf("x16", [128, (K // 128) * T], BF16)
    w16 = p.sbuf("w16", [128, (K // 128) * N], BF16)
    ps = p.psum("ps", [128, 512])
    ps_t = p.psum("ps_t", [128, 128])
    ost = p.sbuf("ost", [128, 512], F32)
    load_fm(p, x32, x_h, K, T, sb16=x16)
    load_w16(p, w16, w_h, K, N, stage)
    b_sb = None
    if bias:
        b_h = p.hbm_in("b", [N // 128, 128])
        b_sb = col_tiles(p, "b_sb", b_h, stage, ps_t, com)
    KD = K // 128
    for nch in range(N // 128):
        for t0 in range(0, T, 512):
            for kc in range(KD):
                p.mm(ps[:, 0:512], w16[:, kc * N + nch * 128: kc * N + nch * 128 + 128],
                     x16[:, kc * T + t0: kc * T + t0 + 512], start=(kc == 0), stop=(kc == KD - 1))
            p.act(ost[:, 0:512], ps[:, 0:512], AF.Identity,
                  bias=(b_sb[:, nch:nch + 1] if bias else 0.0))
            p.dma(o_h[nch * 128:(nch + 1) * 128, t0:t0 + 512], ost[:, 0:512])
    return p.build()


def build_l0_attn():
    p = Prog()
    com = setup_common(p)
    q_h = p.hbm_in("q", [1024, T])
    k_h = p.hbm_in("k", [1024, 8192])
    vt_h = p.hbm_in("vt", [8192, 1024])
    x_h = p.hbm_in("x", [1024, T])
    o_h = p.hbm_out("o", [1024, T])
    SC = 1.0 / 32.0
    stage = p.sbuf("stage", [128, 2048], F32)
    q16 = p.sbuf("q16", [128, FD * T], BF16)
    k16 = p.sbuf("k16", [128, FD * 2048], BF16)
    vt16 = p.sbuf("vt16", [128, 16 * 1024], BF16)
    p16 = p.sbuf("p16", [128, 4 * 512], BF16)
    oacc = p.sbuf("oacc", [128, FD * T], F32)
    lsb = p.sbuf("lsb", [1, T], F32)
    qn = p.sbuf("qn", [1, T], F32)
    nrow16 = p.sbuf("nrow16", [1, T], BF16)
    mk = p.sbuf("mk", [1, 1], F32)
    sq = p.sbuf("sq", [128, 512], F32)
    r1 = p.sbuf("r1", [1, 512], F32)
    r2 = p.sbuf("r2", [1, 512], F32)
    ps = p.psum("ps", [128, 512])
    psav = p.psum("psav", [128, 512])
    ps1 = p.psum("ps1", [1, 512])
    psb = p.psum("psb", [128, 512])
    ps_s1 = p.psum("pss1", [1, 512])
    ps_s2 = p.psum("pss2", [1, 512])
    p.g("memset", qn[:], 0.0)
    for c in range(FD):
        p.dma(stage[:, 0:T], q_h[c * 128:(c + 1) * 128, :])
        p.act(q16[:, c * T:(c + 1) * T], stage[:, 0:T], AF.Copy, scale=SC)
        for t0 in range(0, T, 512):
            p.act(sq[:, 0:512], stage[:, t0:t0 + 512], AF.Square, scale=SC)
            p.mm(ps_s1[0:1, :], com["ones_col"], sq[:, 0:512], start=True, stop=True)
            p.v("tensor_add", qn[0:1, t0:t0 + 512], qn[0:1, t0:t0 + 512], ps_s1[0:1, :])
    p.act(qn[0:1, :], qn[0:1, :], AF.Sqrt)
    p.g("memset", mk[:], 0.0)
    p.g("memset", lsb[:], 0.0)
    for tk0 in range(0, 8192, 2048):
        for c in range(FD):
            p.dma(stage[:, 0:2048], k_h[c * 128:(c + 1) * 128, tk0:tk0 + 2048])
            for t0 in range(0, 2048, 512):
                p.act(sq[:, 0:512], stage[:, t0:t0 + 512], AF.Square)
                p.mm(ps_s2[0:1, :], com["ones_col"], sq[:, 0:512], start=True, stop=True)
                p.v("tensor_add", lsb[0:1, t0:t0 + 512], lsb[0:1, t0:t0 + 512], ps_s2[0:1, :])
        p.v("reduce_max", r1[0:1, 0:1], lsb[0:1, 0:2048], AX.X)
        p.v("tensor_max", mk[:], mk[:], r1[0:1, 0:1])
        p.g("memset", lsb[:], 0.0)
    p.act(mk[:], mk[:], AF.Sqrt)
    p.v("tensor_scalar", qn[0:1, :], qn[0:1, :], mk[0:1, 0:1], -1.0, ALU.mult, ALU.mult)
    p.v("tensor_copy", nrow16[0:1, :], qn[0:1, :])
    p.g("memset", oacc[:], 0.0)
    p.g("memset", lsb[:], 0.0)
    for tk0 in range(0, 8192, 2048):
        for c in range(FD):
            p.dma(stage[:, 0:2048], k_h[c * 128:(c + 1) * 128, tk0:tk0 + 2048])
            p.v("tensor_copy", k16[:, c * 2048:(c + 1) * 2048], stage[:, 0:2048])
        for pt in range(16):
            p.dma(stage[:, 0:1024], vt_h[tk0 + pt * 128: tk0 + (pt + 1) * 128, :])
            p.v("tensor_copy", vt16[:, pt * 1024:(pt + 1) * 1024], stage[:, 0:1024])
        for t0 in range(0, T, 512):
            for grp in range(4):
                for j in range(4):
                    tkc = grp * 4 + j
                    for c in range(FD):
                        p.mm(ps[:, 0:512], k16[:, c * 2048 + tkc * 128: c * 2048 + tkc * 128 + 128],
                             q16[:, c * T + t0: c * T + t0 + 512], start=(c == 0), stop=False)
                    p.mm(ps[:, 0:512], com["ones_row16"], nrow16[0:1, t0:t0 + 512],
                         start=False, stop=True)
                    p.act(p16[:, j * 512:(j + 1) * 512], ps[:, 0:512], AF.Exp)
                for j in range(4):
                    p.mm(ps1[0:1, :], com["ones_col16"], p16[:, j * 512:(j + 1) * 512],
                         start=(j == 0), stop=(j == 3))
                p.v("tensor_add", lsb[0:1, t0:t0 + 512], lsb[0:1, t0:t0 + 512], ps1[0:1, :])
                for ech in range(FD):
                    for j in range(4):
                        pt = grp * 4 + j
                        p.mm(psav[:, 0:512], vt16[:, pt * 1024 + ech * 128: pt * 1024 + ech * 128 + 128],
                             p16[:, j * 512:(j + 1) * 512], start=(j == 0), stop=(j == 3))
                    p.v("tensor_add", oacc[:, ech * T + t0: ech * T + t0 + 512],
                        oacc[:, ech * T + t0: ech * T + t0 + 512], psav[:, 0:512])
    p.v("reciprocal", lsb[0:1, :], lsb[0:1, :])
    for c in range(FD):
        p.dma(stage[:, 0:T], x_h[c * 128:(c + 1) * 128, :])
        for t0 in range(0, T, 512):
            sl = slice(c * T + t0, c * T + t0 + 512)
            p.mm(psb[:, 0:512], com["ones_row"], lsb[0:1, t0:t0 + 512], start=True, stop=True)
            p.v("tensor_mul", oacc[:, sl], oacc[:, sl], psb[:, 0:512])
            p.v("tensor_add", oacc[:, sl], oacc[:, sl], stage[:, t0:t0 + 512])
    emit_ln_dma(p, com, oacc, o_h, T, psb, ps, ps_s1, ps_s2, sq, r1, r2, sq)
    return p.build()


def fp4_quant(p, x32, ncols, sq, sgn, qa, bm, tmp):
    nb = ncols // 16
    x3 = x32.rearrange("p (b s) -> p b s", s=16)
    p.v("reduce_max", bm[:, 0:nb], x3, AX.X, apply_absolute_value=True)
    p.v("tensor_scalar", bm[:, 0:nb], bm[:, 0:nb], 1.0 / 6.0, 1e-12, ALU.mult, ALU.max)
    p.v("reciprocal", tmp[:, 0:nb], bm[:, 0:nb])
    p.act(sgn[:, 0:ncols], x32, AF.Sign)
    p.act(sq[:, 0:ncols], x32, AF.Abs)
    p.v("tensor_mul", sq[:, 0:ncols].rearrange("p (b s) -> p b s", s=16),
        sq[:, 0:ncols].rearrange("p (b s) -> p b s", s=16), bc(tmp[:, 0:nb], 16))
    first = True
    for (th, vl) in ((0.25, 0.5), (0.75, 0.5), (1.25, 0.5), (1.75, 0.5),
                     (2.5, 1.0), (3.5, 1.0), (5.0, 2.0)):
        if first:
            p.v("tensor_scalar", qa[:, 0:ncols], sq[:, 0:ncols], th, vl, ALU.is_gt, ALU.mult)
            first = False
        else:
            p.v("tensor_scalar", x32, sq[:, 0:ncols], th, vl, ALU.is_gt, ALU.mult)
            p.v("tensor_add", qa[:, 0:ncols], qa[:, 0:ncols], x32)
    p.v("tensor_mul", qa[:, 0:ncols], qa[:, 0:ncols], sgn[:, 0:ncols])
    p.v("tensor_mul", x32.rearrange("p (b s) -> p b s", s=16),
        qa[:, 0:ncols].rearrange("p (b s) -> p b s", s=16), bc(bm[:, 0:nb], 16))


def build_ffn0a():
    I = 2816
    p = Prog()
    com = setup_common(p)
    xtm_h = p.hbm_in("xtm", [T, 1024])
    wg_h = p.hbm_in("wg", [I, 1024])
    wu_h = p.hbm_in("wu", [I, 1024])
    htm_h = p.hbm_out("htm", [T, I])
    stage = p.sbuf("stage", [128, 2816], F32)
    sq = p.sbuf("sq", [128, 2816], F32)
    sgn = p.sbuf("sgn", [128, 2816], F32)
    qa = p.sbuf("qa", [128, 2816], F32)
    bm = p.sbuf("bm", [128, 176], F32)
    tmpq = p.sbuf("tmpq", [128, 176], F32)
    xq16 = p.sbuf("xq16", [128, FD * T], BF16)
    wg16 = p.sbuf("wg16", [128, FD * I], BF16)
    wu16 = p.sbuf("wu16", [128, FD * I], BF16)
    gt = p.sbuf("gt", [128, 512], F32)
    htm = p.sbuf("htmb", [128, 2816], F32)
    ps = p.psum("ps", [128, 512])
    ps2 = p.psum("ps2", [128, 512])
    ps_t = p.psum("ps_t", [128, 128])

    def quant_to(dst16, hbm, rows, cols, stride):
        for pt in range(rows // 128):
            p.dma(stage[:, 0:cols], hbm[pt * 128:(pt + 1) * 128, :])
            fp4_quant(p, stage[:, 0:cols], cols, sq, sgn, qa, bm, tmpq)
            for cc in range(cols // 128):
                p.trans(ps_t[:, :], stage[:, cc * 128:(cc + 1) * 128], com["id32"])
                p.v("tensor_copy", dst16[:, cc * stride + pt * 128: cc * stride + pt * 128 + 128],
                    ps_t[:, :])

    quant_to(xq16, xtm_h, T, 1024, T)
    quant_to(wg16, wg_h, I, 1024, I)
    quant_to(wu16, wu_h, I, 1024, I)
    widths = [(0, 512), (512, 512), (1024, 512), (1536, 512), (2048, 512), (2560, 256)]
    for pt in range(16):
        for (n0, w) in widths:
            for kc in range(FD):
                p.mm(ps[:, 0:w], xq16[:, kc * T + pt * 128: kc * T + pt * 128 + 128],
                     wg16[:, kc * I + n0: kc * I + n0 + w], start=(kc == 0), stop=(kc == FD - 1))
            p.act(gt[:, 0:w], ps[:, 0:w], AF.Silu)
            for kc in range(FD):
                p.mm(ps2[:, 0:w], xq16[:, kc * T + pt * 128: kc * T + pt * 128 + 128],
                     wu16[:, kc * I + n0: kc * I + n0 + w], start=(kc == 0), stop=(kc == FD - 1))
            p.v("tensor_mul", htm[:, n0:n0 + w], gt[:, 0:w], ps2[:, 0:w])
        p.dma(htm_h[pt * 128:(pt + 1) * 128, :], htm[:, 0:2816])
    return p.build()


def build_ffn0b():
    I = 2816
    p = Prog()
    com = setup_common(p)
    htm_h = p.hbm_in("htm", [T, I])
    wd_h = p.hbm_in("wd", [1024, I])
    hq_h = p.hbm_out("hq", [I, T])
    wdq_h = p.hbm_out("wdq", [I, 1024])
    stage = p.sbuf("stage", [128, 2816], F32)
    sq = p.sbuf("sq", [128, 2816], F32)
    sgn = p.sbuf("sgn", [128, 2816], F32)
    qa = p.sbuf("qa", [128, 2816], F32)
    bm = p.sbuf("bm", [128, 176], F32)
    tmpq = p.sbuf("tmpq", [128, 176], F32)
    ost = p.sbuf("ost", [128, 128], F32)
    ps_t = p.psum("ps_t", [128, 128])

    def quant_T(hbm_in, hbm_out, rows, cols):
        for pt in range(rows // 128):
            p.dma(stage[:, 0:cols], hbm_in[pt * 128:(pt + 1) * 128, :])
            fp4_quant(p, stage[:, 0:cols], cols, sq, sgn, qa, bm, tmpq)
            for cc in range(cols // 128):
                p.trans(ps_t[:, :], stage[:, cc * 128:(cc + 1) * 128], com["id32"])
                p.v("tensor_copy", ost[:, :], ps_t[:, :])
                p.dma(hbm_out[cc * 128:(cc + 1) * 128, pt * 128:(pt + 1) * 128], ost[:, :])

    quant_T(htm_h, hq_h, T, I)
    quant_T(wd_h, wdq_h, 1024, I)
    return p.build()


def build_ffn0c():
    I = 2816
    p = Prog()
    com = setup_common(p)
    x_h = p.hbm_in("x", [1024, T])
    hq_h = p.hbm_in("hq", [I, T])
    wdq_h = p.hbm_in("wdq", [I, 1024])
    o_h = p.hbm_out("o", [1024, T])
    stage = p.sbuf("stage", [128, 2048], F32)
    racc = p.sbuf("racc", [128, FD * T], F32)
    h16 = p.sbuf("h16", [128, 22 * 512], BF16)
    wdT16 = p.sbuf("wdT16", [128, 22 * 1024], BF16)
    sq = p.sbuf("sq", [128, 512], F32)
    r1 = p.sbuf("r1", [1, 512], F32)
    r2 = p.sbuf("r2", [1, 512], F32)
    ps = p.psum("ps", [128, 512])
    ps2 = p.psum("ps2", [128, 512])
    ps_s1 = p.psum("pss1", [1, 512])
    ps_s2 = p.psum("pss2", [1, 512])
    for kc in range(22):
        p.dma(stage[:, 0:1024], wdq_h[kc * 128:(kc + 1) * 128, :])
        p.v("tensor_copy", wdT16[:, kc * 1024:(kc + 1) * 1024], stage[:, 0:1024])
    for c in range(FD):
        p.dma(racc[:, c * T:(c + 1) * T], x_h[c * 128:(c + 1) * 128, :])
    for t0 in range(0, T, 512):
        for kc in range(22):
            p.dma(stage[:, 0:512], hq_h[kc * 128:(kc + 1) * 128, t0:t0 + 512])
            p.v("tensor_copy", h16[:, kc * 512:(kc + 1) * 512], stage[:, 0:512])
        for nch in range(FD):
            for kc in range(22):
                p.mm(ps[:, 0:512], wdT16[:, kc * 1024 + nch * 128: kc * 1024 + nch * 128 + 128],
                     h16[:, kc * 512:(kc + 1) * 512], start=(kc == 0), stop=(kc == 21))
            p.v("tensor_add", racc[:, nch * T + t0: nch * T + t0 + 512],
                racc[:, nch * T + t0: nch * T + t0 + 512], ps[:, 0:512])
    emit_ln_dma(p, com, racc, o_h, T, ps, ps2, ps_s1, ps_s2, sq, r1, r2, sq)
    return p.build()


def build_l1_attn():
    S = 8192
    p = Prog()
    com = setup_common(p)
    q_h = p.hbm_in("q", [256, S])
    k_h = p.hbm_in("k", [256, S])
    v_h = p.hbm_in("v", [256, S])
    o_h = p.hbm_out("o", [256, S])
    stage = p.sbuf("stage", [64, 2048], F32)
    q16 = p.sbuf("q16", [64, S], BF16)
    k16 = p.sbuf("k16", [64, S], BF16)
    v16s = p.sbuf("v16s", [64, 2048], BF16)
    vt16 = p.sbuf("vt16", [128, 64 * 65], BF16)
    p16 = p.sbuf("p16", [128, 512], BF16)
    sq = p.sbuf("sq", [64, 512], F32)
    qn = p.sbuf("qn", [1, S], F32)
    rowb = p.sbuf("rowb", [1, 512], F32)
    row16 = p.sbuf("row16", [1, 512], BF16)
    nm16 = p.sbuf("nm16", [1, S], BF16)
    mk = p.sbuf("mk", [1, 1], F32)
    r1 = p.sbuf("r1", [1, 512], F32)
    lrec = p.sbuf("lrec", [1, 512], F32)
    osb = p.sbuf("osb", [64, 512], F32)
    mrt = p.sbuf("mrt", [128, 512], F32)
    ps = p.psum("ps", [128, 512])
    psav = p.psum("psav", [128, 512])
    pst = p.psum("pst", [128, 64], BF16)
    psb = p.psum("psb", [64, 512])
    ps1 = p.psum("ps1", [1, 512])
    for hl in range(4):
        r0 = hl * 64
        p.g("memset", mk[:], 0.0)
        for s0 in range(0, S, 2048):
            p.dma(stage[:, :], k_h[r0:r0 + 64, s0:s0 + 2048])
            p.v("tensor_copy", k16[:, s0:s0 + 2048], stage[:, :])
            for t0 in range(0, 2048, 512):
                p.act(sq[:, :], stage[:, t0:t0 + 512], AF.Square)
                p.mm(ps1[0:1, :], com["ones_col"][0:64, :], sq[:, :], start=True, stop=True)
                p.v("reduce_max", r1[0:1, 0:1], ps1[0:1, :], AX.X)
                p.v("tensor_max", mk[:], mk[:], r1[0:1, 0:1])
        p.act(mk[:], mk[:], AF.Sqrt)
        p.g("memset", vt16[:], 1.0)
        for s0 in range(0, S, 2048):
            p.dma(stage[:, :], v_h[r0:r0 + 64, s0:s0 + 2048])
            p.v("tensor_copy", v16s[:, :], stage[:, :])
            for j in range(16):
                jj = s0 // 128 + j
                p.trans(pst[:, 0:64], v16s[:, j * 128:(j + 1) * 128], com["id16"][0:64, 0:64])
                p.v("tensor_copy", vt16[:, jj * 65: jj * 65 + 64], pst[:, 0:64])
        for s0 in range(0, S, 2048):
            p.dma(stage[:, :], q_h[r0:r0 + 64, s0:s0 + 2048])
            p.act(q16[:, s0:s0 + 2048], stage[:, :], AF.Copy, scale=0.125)
            for t0 in range(0, 2048, 512):
                p.act(sq[:, :], stage[:, t0:t0 + 512], AF.Square, scale=0.125)
                p.mm(ps1[0:1, :], com["ones_col"][0:64, :], sq[:, :], start=True, stop=True)
                p.act(qn[0:1, s0 + t0:s0 + t0 + 512], ps1[0:1, :], AF.Sqrt)
        p.v("tensor_scalar", qn[0:1, :], qn[0:1, :], mk[0:1, 0:1], -1.0, ALU.mult, ALU.mult)
        p.v("tensor_copy", nm16[0:1, :], qn[0:1, :])
        p.v("tensor_scalar_mul", rowb[0:1, :], com["frow"][0:1, :], -1.0)
        for t0 in range(0, S, 512):
            j0s = [j for j in range(0, t0 + 512, 128)]
            for (ji, j0) in enumerate(j0s):
                p.mm(ps[:, 0:512], k16[:, j0:j0 + 128], q16[:, t0:t0 + 512],
                     start=True, stop=False)
                p.v("tensor_scalar_add", row16[0:1, :], rowb[0:1, :], float(j0 - t0))
                p.mm(ps[:, 0:512], com["ones_row16"], row16[0:1, :], start=False, stop=False)
                p.mm(ps[:, 0:512], com["ones_row16"], nm16[0:1, t0:t0 + 512],
                     start=False, stop=True)
                if j0 + 127 > t0:
                    p.g("iota", com["idx"][:, 0:512], [[1, 512]], base=t0 - j0,
                        channel_multiplier=-1)
                    p.v("tensor_scalar", mrt[:, 0:512], com["idx"][:, 0:512], 0, -1e9,
                        ALU.is_lt, ALU.mult)
                    p.v("tensor_add", mrt[:, 0:512], mrt[:, 0:512], ps[:, 0:512])
                    p.act(p16[:, 0:512], mrt[:, 0:512], AF.Exp, bias=com["pcol"])
                else:
                    p.act(p16[:, 0:512], ps[:, 0:512], AF.Exp, bias=com["pcol"])
                p.mm(psav[0:65, :], vt16[:, (j0 // 128) * 65:(j0 // 128) * 65 + 65],
                     p16[:, 0:512], start=(ji == 0), stop=(ji == len(j0s) - 1))
            p.v("tensor_copy", osb[:, 0:512], psav[0:64, :])
            p.v("tensor_copy", lrec[0:1, :], psav[64:65, :])
            p.v("reciprocal", lrec[0:1, :], lrec[0:1, :])
            p.mm(psb[0:64, :], com["ones_row"][0:1, 0:64], lrec[0:1, :], start=True, stop=True)
            p.v("tensor_mul", osb[:, 0:512], osb[:, 0:512], psb[0:64, :])
            p.dma(o_h[r0:r0 + 64, t0:t0 + 512], osb[:, 0:512])
    return p.build()


def build_l1_moe():
    HW = 2050
    p = Prog()
    com = setup_common(p)
    x2_h = p.hbm_in("x2", [1024, HW])
    oat_h = p.hbm_in("oat", [1024, HW])
    msk_h = p.hbm_in("msk", [1, HW])
    wo_h = p.hbm_in("wo", [1024, 1024])
    bo_h = p.hbm_in("bo", [8, 128])
    rw_h = p.hbm_in("rw", [1024, 16])
    rb_h = p.hbm_in("rb", [1, 16])
    eg_h = p.hbm_in("eg", [15, 1024, 512])
    eu_h = p.hbm_in("eu", [15, 1024, 512])
    ed_h = p.hbm_in("ed", [15, 512, 1024])
    cw_h = p.hbm_in("cw", [3, 1024, 512])
    su_h = p.hbm_in("su", [1024, 512])
    sd_h = p.hbm_in("sd", [512, 1024])
    o_h = p.hbm_out("o", [1024, T])
    widths = [(0, 512), (512, 512), (1024, 512), (1536, 512), (2048, 2)]
    stage = p.sbuf("stage", [128, 1024], F32)
    racc = p.sbuf("racc", [128, FD * HW], F32)
    x316 = p.sbuf("x316", [128, FD * HW], BF16)
    msk16 = p.sbuf("msk16", [1, HW], F32)
    wo16 = p.sbuf("wo16", [128, FD * 1024], BF16)
    rw16 = p.sbuf("rw16", [128, FD * 16], BF16)
    rb16 = p.sbuf("rb16", [1, 16], BF16)
    eg16 = p.sbuf("eg16", [128, FD * 512], BF16)
    eu16 = p.sbuf("eu16", [128, FD * 512], BF16)
    ed16 = p.sbuf("ed16", [128, 4 * 1024], BF16)
    sc32 = p.sbuf("sc32", [128, 2048], F32)
    sc16 = p.sbuf("sc16", [128, 512], BF16)
    comb = p.sbuf("comb", [128, 256], F32)
    combt = p.sbuf("combt", [16, 2048], BF16)
    r0row = p.sbuf("r0row", [1, 2048], BF16)
    gt32 = p.sbuf("gt32", [128, 512], F32)
    h16 = p.sbuf("h16", [128, 2048], BF16)
    sq = p.sbuf("sq", [128, 512], F32)
    r1 = p.sbuf("r1", [1, 512], F32)
    r2 = p.sbuf("r2", [1, 512], F32)
    pcA = p.sbuf("pcA", [128, 1], F32)
    pcB = p.sbuf("pcB", [128, 1], F32)
    pcC = p.sbuf("pcC", [128, 1], F32)
    pcD = p.sbuf("pcD", [128, 1], F32)
    e32 = p.sbuf("e32", [128, 16], F32)
    t32 = p.sbuf("t32", [128, 16], F32)
    ps = p.psum("ps", [128, 512])
    ps2 = p.psum("ps2", [128, 512])
    ps_t = p.psum("ps_t", [128, 128])
    ps_s1 = p.psum("pss1", [1, 512])
    ps_s2 = p.psum("pss2", [1, 512])
    load_w16(p, wo16, wo_h, 1024, 1024, stage)
    bo_sb = col_tiles(p, "bo_sb", bo_h, stage, ps_t, com)
    oatst = p.sbuf("oatst", [128, 512], BF16)
    for nch in range(FD):
        p.dma(racc[:, nch * HW:(nch + 1) * HW], x2_h[nch * 128:(nch + 1) * 128, :])
        for (t0, w) in widths:
            for kc in range(FD):
                p.dma(stage[:, 0:w], oat_h[kc * 128:(kc + 1) * 128, t0:t0 + w])
                p.v("tensor_copy", oatst[:, 0:w], stage[:, 0:w])
                p.mm(ps[:, 0:w], wo16[:, kc * 1024 + nch * 128: kc * 1024 + nch * 128 + 128],
                     oatst[:, 0:w], start=(kc == 0), stop=(kc == FD - 1))
            p.act(gt32[:, 0:w], ps[:, 0:w], AF.Identity, bias=bo_sb[:, nch:nch + 1])
            p.v("tensor_add", racc[:, nch * HW + t0: nch * HW + t0 + w],
                racc[:, nch * HW + t0: nch * HW + t0 + w], gt32[:, 0:w])
    for (t0, w) in widths:
        for c in range(FD):
            sl = slice(c * HW + t0, c * HW + t0 + w)
            p.mm(ps_s1[0:1, 0:w], com["ones_col"], racc[:, sl], start=(c == 0), stop=(c == FD - 1))
        for c in range(FD):
            sl = slice(c * HW + t0, c * HW + t0 + w)
            p.act(sq[:, 0:w], racc[:, sl], AF.Square)
            p.mm(ps_s2[0:1, 0:w], com["ones_col"], sq[:, 0:w], start=(c == 0), stop=(c == FD - 1))
        p.v("tensor_scalar_mul", r1[0:1, 0:w], ps_s1[0:1, 0:w], 1.0 / D)
        p.v("tensor_mul", r2[0:1, 0:w], r1[0:1, 0:w], r1[0:1, 0:w])
        p.v("scalar_tensor_tensor", r2[0:1, 0:w], ps_s2[0:1, 0:w], 1.0 / D, r2[0:1, 0:w],
            ALU.mult, ALU.subtract)
        p.v("tensor_scalar_add", r2[0:1, 0:w], r2[0:1, 0:w], 1e-5)
        p.act(r2[0:1, 0:w], r2[0:1, 0:w], AF.Sqrt)
        p.v("reciprocal", r2[0:1, 0:w], r2[0:1, 0:w])
        p.mm(ps[:, 0:w], com["ones_row"], r1[0:1, 0:w], start=True, stop=True)
        p.mm(ps2[:, 0:w], com["ones_row"], r2[0:1, 0:w], start=True, stop=True)
        p.dma(msk16[0:1, t0:t0 + w], msk_h[0:1, t0:t0 + w])
        for c in range(FD):
            sl = slice(c * HW + t0, c * HW + t0 + w)
            p.v("tensor_sub", gt32[:, 0:w], racc[:, sl], ps[:, 0:w])
            p.v("tensor_mul", x316[:, sl], gt32[:, 0:w], ps2[:, 0:w])
    for (t0, w) in widths:
        p.mm(ps[:, 0:w], com["ones_row"], msk16[0:1, t0:t0 + w], start=True, stop=True)
        p.v("tensor_copy", sc16[:, 0:w], ps[:, 0:w])
        for c in range(FD):
            sl = slice(c * HW + t0, c * HW + t0 + w)
            p.v("tensor_mul", x316[:, sl], x316[:, sl], sc16[:, 0:w])
    load_w16(p, rw16, rw_h, 1024, 16, stage)
    p.dma(stage[0:1, 0:16], rb_h[0:1, :])
    p.v("tensor_copy", rb16[0:1, :], stage[0:1, 0:16])
    for tt in range(16):
        for kc in range(FD):
            p.mm(ps[:, 0:16], x316[:, kc * HW + 2 + tt * 128: kc * HW + 2 + tt * 128 + 128],
                 rw16[:, kc * 16: kc * 16 + 16], start=(kc == 0), stop=False)
        p.mm(ps[:, 0:16], com["ones_row16"], rb16[0:1, :], start=False, stop=True)
        p.v("reduce_max", pcA[:, 0:1], ps[:, 0:16], AX.X)
        p.v("tensor_scalar_mul", pcA[:, 0:1], pcA[:, 0:1], -1.0)
        p.act(e32[:, :], ps[:, 0:16], AF.Exp, bias=pcA[:, 0:1])
        p.v("reduce_sum", pcB[:, 0:1], e32[:, :], AX.X)
        p.v("reciprocal", pcB[:, 0:1], pcB[:, 0:1])
        p.v("tensor_scalar_mul", e32[:, :], e32[:, :], pcB[:, 0:1])
        p.v("reduce_max", pcA[:, 0:1], e32[:, :], AX.X)
        p.v("tensor_scalar", t32[:, :], e32[:, :], pcA[:, 0:1], None, ALU.is_equal)
        p.v("tensor_mul", comb[:, tt * 16:(tt + 1) * 16], t32[:, :], e32[:, :])
        p.v("tensor_sub", e32[:, :], e32[:, :], comb[:, tt * 16:(tt + 1) * 16])
        p.v("reduce_max", pcB[:, 0:1], e32[:, :], AX.X)
        p.v("tensor_add", pcC[:, 0:1], pcA[:, 0:1], pcB[:, 0:1])
        p.v("tensor_scalar_add", pcC[:, 0:1], pcC[:, 0:1], 1e-6)
        p.v("reciprocal", pcC[:, 0:1], pcC[:, 0:1])
        p.v("tensor_mul", pcD[:, 0:1], pcA[:, 0:1], pcC[:, 0:1])
        p.v("tensor_mul", pcC[:, 0:1], pcB[:, 0:1], pcC[:, 0:1])
        p.v("tensor_scalar", e32[:, :], e32[:, :], pcB[:, 0:1], None, ALU.is_equal)
        p.v("tensor_scalar_mul", e32[:, :], e32[:, :], pcC[:, 0:1])
        p.v("tensor_scalar_mul", t32[:, :], t32[:, :], pcD[:, 0:1])
        p.v("tensor_add", t32[:, :], t32[:, :], e32[:, :])
        p.v("tensor_copy", comb[:, tt * 16:(tt + 1) * 16], t32[:, :])
        p.trans(ps_t[0:16, 0:128], comb[:, tt * 16:(tt + 1) * 16], com["id32"])
        p.v("tensor_copy", combt[0:16, tt * 128:(tt + 1) * 128], ps_t[0:16, 0:128])

    def bcast_scale(e):
        p.dma(r0row[0:1, :], combt[e:e + 1, :])
        for t0 in range(0, T, 512):
            p.mm(ps[:, 0:512], com["ones_row16"], r0row[0:1, t0:t0 + 512], start=True, stop=True)
            p.v("tensor_copy", sc32[:, t0:t0 + 512], ps[:, 0:512])

    bcast_scale(15)
    for c in range(FD):
        for t0 in range(0, T, 512):
            p.v("tensor_copy", gt32[:, 0:512], x316[:, c * HW + 2 + t0: c * HW + 2 + t0 + 512])
            p.v("tensor_mul", racc[:, c * T + t0: c * T + t0 + 512], gt32[:, 0:512],
                sc32[:, t0:t0 + 512])
    for e in range(15):
        bcast_scale(e)
        for kc in range(FD):
            p.dma(stage[:, 0:512], eg_h[e, kc * 128:(kc + 1) * 128, :])
            p.v("tensor_copy", eg16[:, kc * 512:(kc + 1) * 512], stage[:, 0:512])
            p.dma(stage[:, 0:512], eu_h[e, kc * 128:(kc + 1) * 128, :])
            p.v("tensor_copy", eu16[:, kc * 512:(kc + 1) * 512], stage[:, 0:512])
        for kc in range(4):
            p.dma(stage[:, 0:1024], ed_h[e, kc * 128:(kc + 1) * 128, :])
            p.v("tensor_copy", ed16[:, kc * 1024:(kc + 1) * 1024], stage[:, 0:1024])
        for t0 in range(0, T, 512):
            for nch in range(4):
                for kc in range(FD):
                    p.mm(ps[:, 0:512], eg16[:, kc * 512 + nch * 128: kc * 512 + nch * 128 + 128],
                         x316[:, kc * HW + 2 + t0: kc * HW + 2 + t0 + 512],
                         start=(kc == 0), stop=(kc == FD - 1))
                p.act(gt32[:, 0:512], ps[:, 0:512], AF.Silu)
                for kc in range(FD):
                    p.mm(ps2[:, 0:512], eu16[:, kc * 512 + nch * 128: kc * 512 + nch * 128 + 128],
                         x316[:, kc * HW + 2 + t0: kc * HW + 2 + t0 + 512],
                         start=(kc == 0), stop=(kc == FD - 1))
                p.v("tensor_mul", h16[:, nch * 512:(nch + 1) * 512], gt32[:, 0:512], ps2[:, 0:512])
            for oc in range(FD):
                for kc in range(4):
                    p.mm(ps[:, 0:512], ed16[:, kc * 1024 + oc * 128: kc * 1024 + oc * 128 + 128],
                         h16[:, kc * 512:(kc + 1) * 512], start=(kc == 0), stop=(kc == 3))
                p.v("tensor_mul", gt32[:, 0:512], ps[:, 0:512], sc32[:, t0:t0 + 512])
                p.v("tensor_add", racc[:, oc * T + t0: oc * T + t0 + 512],
                    racc[:, oc * T + t0: oc * T + t0 + 512], gt32[:, 0:512])
    cw16 = eg16
    su16 = eu16
    sd16 = ed16
    for k in range(3):
        for kc in range(FD):
            p.dma(stage[:, 0:512], cw_h[k, kc * 128:(kc + 1) * 128, :])
            p.v("tensor_copy", cw16[:, kc * 512:(kc + 1) * 512] if k == 0 else
                (eu16[:, kc * 512:(kc + 1) * 512] if k == 1 else sd16[:, kc * 512:(kc + 1) * 512]),
                stage[:, 0:512])
    su16b = p.sbuf("su16b", [128, FD * 512], BF16)
    for kc in range(FD):
        p.dma(stage[:, 0:512], su_h[kc * 128:(kc + 1) * 128, :])
        p.v("tensor_copy", su16b[:, kc * 512:(kc + 1) * 512], stage[:, 0:512])
    sd16b = p.sbuf("sd16b", [128, 4 * 1024], BF16)
    for kc in range(4):
        p.dma(stage[:, 0:1024], sd_h[kc * 128:(kc + 1) * 128, :])
        p.v("tensor_copy", sd16b[:, kc * 1024:(kc + 1) * 1024], stage[:, 0:1024])
    taps = [cw16, eu16, sd16]
    for t0 in range(0, T, 512):
        for nch in range(4):
            first = True
            for k in range(3):
                for kc in range(FD):
                    p.mm(ps[:, 0:512], taps[k][:, kc * 512 + nch * 128: kc * 512 + nch * 128 + 128],
                         x316[:, kc * HW + t0 + k: kc * HW + t0 + k + 512],
                         start=first, stop=(k == 2 and kc == FD - 1))
                    first = False
            p.act(gt32[:, 0:512], ps[:, 0:512], AF.Silu)
            for kc in range(FD):
                p.mm(ps2[:, 0:512], su16b[:, kc * 512 + nch * 128: kc * 512 + nch * 128 + 128],
                     x316[:, kc * HW + 2 + t0: kc * HW + 2 + t0 + 512],
                     start=(kc == 0), stop=(kc == FD - 1))
            p.v("tensor_mul", h16[:, nch * 512:(nch + 1) * 512], gt32[:, 0:512], ps2[:, 0:512])
        for oc in range(FD):
            for kc in range(4):
                p.mm(ps[:, 0:512], sd16b[:, kc * 1024 + oc * 128: kc * 1024 + oc * 128 + 128],
                     h16[:, kc * 512:(kc + 1) * 512], start=(kc == 0), stop=(kc == 3))
            p.v("tensor_copy", gt32[:, 0:512], ps[:, 0:512])
            p.v("tensor_add", racc[:, oc * T + t0: oc * T + t0 + 512],
                racc[:, oc * T + t0: oc * T + t0 + 512], gt32[:, 0:512])
    for c in range(FD):
        for t0 in range(0, T, 512):
            p.v("tensor_copy", gt32[:, 0:512], x316[:, c * HW + 2 + t0: c * HW + 2 + t0 + 512])
            p.v("tensor_add", racc[:, c * T + t0: c * T + t0 + 512],
                racc[:, c * T + t0: c * T + t0 + 512], gt32[:, 0:512])
    emit_ln_dma(p, com, racc, o_h, T, ps, ps2, ps_s1, ps_s2, sq, r1, r2, gt32)
    return p.build()


def emit_rangesin(p, dst, src, k32, kb, ki):
    p.v("tensor_scalar_mul", k32, src, 1.0 / (2 * math.pi))
    p.v("tensor_copy", ki, k32)
    p.v("tensor_copy", k32, ki)
    p.v("scalar_tensor_tensor", dst, k32, -2 * math.pi, src, ALU.mult, ALU.add)
    p.v("tensor_scalar", kb, dst, math.pi, 2 * math.pi, ALU.is_gt, ALU.mult)
    p.v("tensor_sub", dst, dst, kb)
    p.v("tensor_scalar", kb, dst, -math.pi, 2 * math.pi, ALU.is_lt, ALU.mult)
    p.v("tensor_add", dst, dst, kb)
    p.act(dst, dst, AF.Sin)


def build_rope2():
    p = Prog()
    com = setup_common(p)
    qr_h = p.hbm_in("qr", [512, T])
    kr_h = p.hbm_in("kr", [32, T])
    pos_h = p.hbm_in("pos", [1, T])
    qo_h = p.hbm_out("qo", [512, T])
    ko_h = p.hbm_out("ko", [32, T])
    qr32 = p.sbuf("qr32", [128, 4 * T], F32)
    kr1 = p.sbuf("kr1", [16, T], F32)
    kr2 = p.sbuf("kr2", [16, T], F32)
    kr3 = p.sbuf("kr3", [16, T], F32)
    pos = p.sbuf("poss", [1, T], F32)
    inv = p.sbuf("inv", [1, 16], F32)
    sin16 = p.sbuf("sin16", [16, T], F32)
    cos16 = p.sbuf("cos16", [16, T], F32)
    sinr = p.sbuf("sinr", [128, T], F32)
    cosr = p.sbuf("cosr", [128, T], F32)
    istk = p.sbuf("istk", [16, 128], F32)
    k32 = p.sbuf("k32", [16, 512], F32)
    ki = p.sbuf("ki", [16, 512], I32)
    t1 = p.sbuf("t1", [128, T], F32)
    t2 = p.sbuf("t2", [128, T], F32)
    ps = p.psum("ps", [128, 512])
    ps16 = p.psum("ps16", [16, 512])
    for c in range(4):
        p.dma(qr32[:, c * T:(c + 1) * T], qr_h[c * 128:(c + 1) * 128, :])
    p.dma(kr1[:, :], kr_h[0:16, :])
    p.dma(kr2[:, :], kr_h[16:32, :])
    p.dma(pos[:, :], pos_h[:, :])
    p.g("iota", ki[0:1, 0:16], [[1, 16]], channel_multiplier=0)
    p.v("tensor_copy", inv[0:1, :], ki[0:1, 0:16])
    p.act(inv[0:1, :], inv[0:1, :], AF.Exp, scale=-math.log(10000.0) / 16.0)
    kb = p.sbuf("kb", [16, 512], F32)
    ang = p.sbuf("ang", [16, 512], F32)
    for t0 in range(0, T, 512):
        p.mm(ps16[:, :], inv[0:1, :], pos[0:1, t0:t0 + 512], start=True, stop=True)
        p.v("tensor_copy", ang[:, :], ps16[:, :])
        emit_rangesin(p, sin16[:, t0:t0 + 512], ang[:, :], k32[:, 0:512], kb[:, :], ki[:, 0:512])
        p.v("tensor_scalar_add", ang[:, :], ps16[:, :], math.pi / 2.0)
        emit_rangesin(p, cos16[:, t0:t0 + 512], ang[:, :], k32[:, 0:512], kb[:, :], ki[:, 0:512])
    p.g("memset", istk[:, :], 0.0)
    for j in range(8):
        p.v("tensor_copy", istk[0:16, j * 16:(j + 1) * 16], com["id32"][0:16, 0:16])
    for t0 in range(0, T, 512):
        p.mm(ps[:, 0:512], istk[:, :], sin16[:, t0:t0 + 512], start=True, stop=True)
        p.v("tensor_copy", sinr[:, t0:t0 + 512], ps[:, 0:512])
        p.mm(ps[:, 0:512], istk[:, :], cos16[:, t0:t0 + 512], start=True, stop=True)
        p.v("tensor_copy", cosr[:, t0:t0 + 512], ps[:, 0:512])
    for pt in range(2):
        x1 = qr32[:, pt * T:(pt + 1) * T]
        x2 = qr32[:, (pt + 2) * T:(pt + 3) * T]
        p.v("tensor_mul", t1[:, :], x1, cosr[:, :])
        p.v("tensor_mul", t2[:, :], x2, sinr[:, :])
        p.v("tensor_sub", t1[:, :], t1[:, :], t2[:, :])
        p.v("tensor_mul", t2[:, :], x2, cosr[:, :])
        p.v("tensor_mul", x2, x1, sinr[:, :])
        p.v("tensor_add", t2[:, :], t2[:, :], x2)
        p.dma(qo_h[pt * 128:(pt + 1) * 128, :], t1[:, :])
        p.dma(qo_h[(pt + 2) * 128:(pt + 3) * 128, :], t2[:, :])
    p.v("tensor_mul", t1[0:16, :], kr1[:, :], cos16[:, :])
    p.v("tensor_mul", t2[0:16, :], kr2[:, :], sin16[:, :])
    p.v("tensor_sub", t1[0:16, :], t1[0:16, :], t2[0:16, :])
    p.v("tensor_mul", t2[0:16, :], kr2[:, :], cos16[:, :])
    p.v("tensor_mul", kr3[:, :], kr1[:, :], sin16[:, :])
    p.v("tensor_add", t2[0:16, :], t2[0:16, :], kr3[:, :])
    p.dma(ko_h[0:16, :], t1[0:16, :])
    p.dma(ko_h[16:32, :], t2[0:16, :])
    return p.build()


def build_l2_attn():
    S = 8192
    SC = 1.0 / math.sqrt(96.0)
    p = Prog()
    com = setup_common(p)
    q_h = p.hbm_in("q", [384, S])
    k_h = p.hbm_in("k", [384, S])
    v_h = p.hbm_in("v", [256, S])
    c_h = p.hbm_in("c", [1, 4])
    o_h = p.hbm_out("o", [256, S])
    stage = p.sbuf("stage", [96, 2048], F32)
    q16 = p.sbuf("q16", [96, S], BF16)
    k16 = p.sbuf("k16", [96, S], BF16)
    v16s = p.sbuf("v16s", [64, 2048], BF16)
    vt16 = p.sbuf("vt16", [128, 64 * 65], BF16)
    p16 = p.sbuf("p16", [128, 512], BF16)
    msb = p.sbuf("msb", [128, 64], F32)
    mtr = p.sbuf("mtr", [64, 128], F32)
    mrow = p.sbuf("mrow", [1, S], F32)
    mb_sb = p.sbuf("mb_sb", [128, 512], F32)
    mrt = p.sbuf("mrt", [128, 512], F32)
    r128 = p.sbuf("r128", [128, 1], F32)
    c32 = p.sbuf("c32", [1, 4], F32)
    lrec = p.sbuf("lrec", [1, 512], F32)
    osb = p.sbuf("osb", [64, 512], F32)
    ps = p.psum("ps", [128, 512])
    psav = p.psum("psav", [128, 512])
    pst16 = p.psum("pst16", [128, 64], BF16)
    pst32 = p.psum("pst32", [64, 128])
    psb = p.psum("psb", [128, 512])
    p.dma(c32[0:1, :], c_h[0:1, :])
    p.act(c32[0:1, :], c32[0:1, :], AF.Exp)
    for hl in range(4):
        r0q = hl * 96
        r0v = hl * 64
        for s0 in range(0, S, 2048):
            p.dma(stage[:, :], q_h[r0q:r0q + 96, s0:s0 + 2048])
            p.act(q16[:, s0:s0 + 2048], stage[:, :], AF.Copy, scale=SC)
            p.dma(stage[:, :], k_h[r0q:r0q + 96, s0:s0 + 2048])
            p.v("tensor_copy", k16[:, s0:s0 + 2048], stage[:, :])
        p.g("memset", vt16[:], 1.0)
        for s0 in range(0, S, 2048):
            p.dma(stage[0:64, :], v_h[r0v:r0v + 64, s0:s0 + 2048])
            p.v("tensor_copy", v16s[:, :], stage[0:64, :])
            for j in range(16):
                jj = s0 // 128 + j
                p.trans(pst16[:, 0:64], v16s[:, j * 128:(j + 1) * 128], com["id16"][0:64, 0:64])
                p.v("tensor_copy", vt16[:, jj * 65: jj * 65 + 64], pst16[:, 0:64])
        p.g("memset", msb[:], -3e8)
        for tq in range(64):
            for j0 in range(0, tq * 128 + 128, 512):
                p.mm(ps[:, 0:512], q16[:, tq * 128: tq * 128 + 128], k16[:, j0:j0 + 512],
                     start=True, stop=True)
                if j0 + 511 > tq * 128:
                    p.g("iota", com["idx"][:, 0:512], [[-1, 512]], base=tq * 128 - j0,
                        channel_multiplier=1)
                    p.v("tensor_scalar", mrt[:, 0:512], com["idx"][:, 0:512], 0, -1e9,
                        ALU.is_lt, ALU.mult)
                    p.v("tensor_add", mrt[:, 0:512], mrt[:, 0:512], ps[:, 0:512])
                    p.v("reduce_max", r128[:, 0:1], mrt[:, 0:512], AX.X)
                else:
                    p.v("reduce_max", r128[:, 0:1], ps[:, 0:512], AX.X)
                p.v("tensor_max", msb[:, tq:tq + 1], msb[:, tq:tq + 1], r128[:, 0:1])
        p.trans(pst32[:, :], msb[:, 0:64], com["id32"])
        p.v("tensor_copy", mtr[:, :], pst32[:, :])
        p.dma(mrow[0:1, :].rearrange("a (b c) -> a b c", b=64), mtr[:, :])
        for t0 in range(0, S, 512):
            p.mm(psb[:, 0:512], com["ones_row"], mrow[0:1, t0:t0 + 512], start=True, stop=True)
            p.v("tensor_copy", mb_sb[:, :], psb[:, 0:512])
            j0s = list(range(0, t0 + 512, 128))
            for (ji, j0) in enumerate(j0s):
                p.mm(ps[:, 0:512], k16[:, j0:j0 + 128], q16[:, t0:t0 + 512],
                     start=True, stop=True)
                if j0 + 127 > t0:
                    p.g("iota", com["idx"][:, 0:512], [[1, 512]], base=t0 - j0,
                        channel_multiplier=-1)
                    p.v("tensor_scalar", mrt[:, 0:512], com["idx"][:, 0:512], 0, -1e9,
                        ALU.is_lt, ALU.mult)
                    p.v("tensor_add", mrt[:, 0:512], mrt[:, 0:512], ps[:, 0:512])
                    p.v("tensor_sub", mrt[:, 0:512], mrt[:, 0:512], mb_sb[:, :])
                else:
                    p.v("tensor_sub", mrt[:, 0:512], ps[:, 0:512], mb_sb[:, :])
                p.act(p16[:, 0:512], mrt[:, 0:512], AF.Exp)
                p.mm(psav[0:65, :], vt16[:, (j0 // 128) * 65:(j0 // 128) * 65 + 65],
                     p16[:, 0:512], start=(ji == 0), stop=(ji == len(j0s) - 1))
            p.v("tensor_scalar", lrec[0:1, :], psav[64:65, :], c32[0:1, hl:hl + 1], None, ALU.add)
            p.v("reciprocal", lrec[0:1, :], lrec[0:1, :])
            p.v("tensor_copy", osb[:, 0:512], psav[0:64, :])
            p.mm(psb[0:64, :], com["ones_row"][0:1, 0:64], lrec[0:1, :], start=True, stop=True)
            p.v("tensor_mul", osb[:, 0:512], osb[:, 0:512], psb[0:64, :])
            p.dma(o_h[r0v:r0v + 64, t0:t0 + 512], osb[:, 0:512])
    return p.build()


def build_ffn2a():
    p = Prog()
    com = setup_common(p)
    x4_h = p.hbm_in("x4", [1024, T])
    oat_h = p.hbm_in("oat", [1024, T])
    wo_h = p.hbm_in("wo", [1024, 1024])
    w1_h = p.hbm_in("w1", [1024, 4096])
    s1_h = p.hbm_in("s1", [32, 128])
    w2_h = p.hbm_in("w2", [1024, 4096])
    s2_h = p.hbm_in("s2", [32, 128])
    x5_h = p.hbm_out("x5", [1024, T])
    h_h = p.hbm_out("h", [4096, T])
    stage = p.sbuf("stage", [128, 2048], F32)
    racc = p.sbuf("racc", [128, FD * T], F32)
    x516 = p.sbuf("x516", [128, FD * T], BF16)
    oat16 = p.sbuf("oat16", [128, FD * T], BF16)
    wo16 = p.sbuf("wo16", [128, FD * 1024], BF16)
    wsl = p.sbuf("wsl", [128, FD * 128], BF16)
    wsl2 = p.sbuf("wsl2", [128, FD * 128], BF16)
    gt32 = p.sbuf("gt32", [128, 512], F32)
    u32 = p.sbuf("u32", [128, 512], F32)
    h32 = p.sbuf("h32", [128, 512], F32)
    sq = p.sbuf("sq", [128, 512], F32)
    r1 = p.sbuf("r1", [1, 512], F32)
    r2 = p.sbuf("r2", [1, 512], F32)
    ps = p.psum("ps", [128, 512])
    ps2 = p.psum("ps2", [128, 512])
    ps_t = p.psum("ps_t", [128, 128])
    ps_s1 = p.psum("pss1", [1, 512])
    ps_s2 = p.psum("pss2", [1, 512])
    load_w16(p, wo16, wo_h, 1024, 1024, stage)
    for c in range(FD):
        p.dma(stage[:, 0:T], oat_h[c * 128:(c + 1) * 128, :])
        p.v("tensor_copy", oat16[:, c * T:(c + 1) * T], stage[:, 0:T])
    for nch in range(FD):
        p.dma(racc[:, nch * T:(nch + 1) * T], x4_h[nch * 128:(nch + 1) * 128, :])
        for t0 in range(0, T, 512):
            for kc in range(FD):
                p.mm(ps[:, 0:512], wo16[:, kc * 1024 + nch * 128: kc * 1024 + nch * 128 + 128],
                     oat16[:, kc * T + t0: kc * T + t0 + 512], start=(kc == 0), stop=(kc == FD - 1))
            p.v("tensor_add", racc[:, nch * T + t0: nch * T + t0 + 512],
                racc[:, nch * T + t0: nch * T + t0 + 512], ps[:, 0:512])
    emit_ln_dma(p, com, racc, x5_h, T, ps, ps2, ps_s1, ps_s2, sq, r1, r2, h32, out16=x516)
    s1_sb = col_tiles(p, "s1_sb", s1_h, stage, ps_t, com)
    s2_sb = col_tiles(p, "s2_sb", s2_h, stage, ps_t, com)
    for nch in range(32):
        for kc in range(FD):
            p.dma(stage[:, 0:128], w1_h[kc * 128:(kc + 1) * 128, nch * 128:(nch + 1) * 128])
            p.v("tensor_copy", wsl[:, kc * 128:(kc + 1) * 128], stage[:, 0:128])
            p.dma(stage[:, 0:128], w2_h[kc * 128:(kc + 1) * 128, nch * 128:(nch + 1) * 128])
            p.v("tensor_copy", wsl2[:, kc * 128:(kc + 1) * 128], stage[:, 0:128])
        for t0 in range(0, T, 512):
            for kc in range(FD):
                p.mm(ps[:, 0:512], wsl[:, kc * 128:(kc + 1) * 128],
                     x516[:, kc * T + t0: kc * T + t0 + 512], start=(kc == 0), stop=(kc == FD - 1))
            p.act(gt32[:, 0:512], ps[:, 0:512], AF.Silu, scale=s1_sb[:, nch:nch + 1])
            for kc in range(FD):
                p.mm(ps2[:, 0:512], wsl2[:, kc * 128:(kc + 1) * 128],
                     x516[:, kc * T + t0: kc * T + t0 + 512], start=(kc == 0), stop=(kc == FD - 1))
            p.act(u32[:, 0:512], ps2[:, 0:512], AF.Identity, scale=s2_sb[:, nch:nch + 1])
            p.v("tensor_mul", h32[:, 0:512], gt32[:, 0:512], u32[:, 0:512])
            p.dma(h_h[nch * 128:(nch + 1) * 128, t0:t0 + 512], h32[:, 0:512])
    return p.build()


def build_ffn2bc():
    p = Prog()
    com = setup_common(p)
    h_h = p.hbm_in("h", [4096, T])
    wt_h = p.hbm_in("wt", [32, 4096, 128])
    s_h = p.hbm_in("s", [32, 128])
    o_h = p.hbm_out("o", [4096, T])
    stage = p.sbuf("stage", [128, 2048], F32)
    h16 = p.sbuf("h16", [128, 32 * T], BF16)
    w16 = p.sbuf("w16", [128, 32 * 128], BF16)
    o32 = p.sbuf("o32", [128, 512], F32)
    ps = p.psum("ps", [128, 512])
    ps_t = p.psum("ps_t", [128, 128])
    s_sb = col_tiles(p, "s_sb", s_h, stage, ps_t, com)
    for c in range(32):
        p.dma(stage[:, 0:T], h_h[c * 128:(c + 1) * 128, :])
        p.v("tensor_copy", h16[:, c * T:(c + 1) * T], stage[:, 0:T])
    for nch in range(32):
        for kc in range(32):
            p.dma(stage[:, 0:128], wt_h[nch, kc * 128:(kc + 1) * 128, :])
            p.v("tensor_copy", w16[:, kc * 128:(kc + 1) * 128], stage[:, 0:128])
        for t0 in range(0, T, 512):
            for kc in range(32):
                p.mm(ps[:, 0:512], w16[:, kc * 128:(kc + 1) * 128],
                     h16[:, kc * T + t0: kc * T + t0 + 512], start=(kc == 0), stop=(kc == 31))
            p.act(o32[:, 0:512], ps[:, 0:512], AF.Gelu, scale=s_sb[:, nch:nch + 1])
            p.dma(o_h[nch * 128:(nch + 1) * 128, t0:t0 + 512], o32[:, 0:512])
    return p.build()


def build_ffn2d():
    p = Prog()
    com = setup_common(p)
    h_h = p.hbm_in("h", [4096, T])
    wt_h = p.hbm_in("wt", [8, 4096, 128])
    s_h = p.hbm_in("s", [8, 128])
    x5_h = p.hbm_in("x5", [1024, T])
    o_h = p.hbm_out("o", [1024, T])
    stage = p.sbuf("stage", [128, 2048], F32)
    h16 = p.sbuf("h16", [128, 32 * 512], BF16)
    w16 = p.sbuf("w16", [128, 32 * 1024], BF16)
    racc = p.sbuf("racc", [128, FD * T], F32)
    o32 = p.sbuf("o32", [128, 512], F32)
    sq = p.sbuf("sq", [128, 512], F32)
    r1 = p.sbuf("r1", [1, 512], F32)
    r2 = p.sbuf("r2", [1, 512], F32)
    ps = p.psum("ps", [128, 512])
    ps2 = p.psum("ps2", [128, 512])
    ps_t = p.psum("ps_t", [128, 128])
    ps_s1 = p.psum("pss1", [1, 512])
    ps_s2 = p.psum("pss2", [1, 512])
    s_sb = col_tiles(p, "s_sb", s_h, stage, ps_t, com)
    for nch in range(8):
        for kc in range(32):
            p.dma(stage[:, 0:128], wt_h[nch, kc * 128:(kc + 1) * 128, :])
            p.v("tensor_copy", w16[:, kc * 1024 + nch * 128: kc * 1024 + nch * 128 + 128],
                stage[:, 0:128])
    for c in range(FD):
        p.dma(racc[:, c * T:(c + 1) * T], x5_h[c * 128:(c + 1) * 128, :])
    for t0 in range(0, T, 512):
        for kc in range(32):
            p.dma(stage[:, 0:512], h_h[kc * 128:(kc + 1) * 128, t0:t0 + 512])
            p.v("tensor_copy", h16[:, kc * 512:(kc + 1) * 512], stage[:, 0:512])
        for nch in range(8):
            for kc in range(32):
                p.mm(ps[:, 0:512], w16[:, kc * 1024 + nch * 128: kc * 1024 + nch * 128 + 128],
                     h16[:, kc * 512:(kc + 1) * 512], start=(kc == 0), stop=(kc == 31))
            p.act(o32[:, 0:512], ps[:, 0:512], AF.Identity, scale=s_sb[:, nch:nch + 1])
            p.v("tensor_add", racc[:, nch * T + t0: nch * T + t0 + 512],
                racc[:, nch * T + t0: nch * T + t0 + 512], o32[:, 0:512])
    emit_ln_dma(p, com, racc, o_h, T, ps, ps2, ps_s1, ps_s2, sq, r1, r2, o32)
    return p.build()


def build_rope3():
    p = Prog()
    com = setup_common(p)
    q_h = p.hbm_in("q", [1024, T])
    k_h = p.hbm_in("k", [1024, T])
    pos_h = p.hbm_in("pos", [1, T])
    qo_h = p.hbm_out("qo", [1024, T])
    ko_h = p.hbm_out("ko", [1024, T])
    x32 = p.sbuf("x32", [128, FD * T], F32)
    pos = p.sbuf("poss", [1, T], F32)
    inv = p.sbuf("inv", [1, 64], F32)
    sin64 = p.sbuf("sin64", [64, T], F32)
    cos64 = p.sbuf("cos64", [64, T], F32)
    k32 = p.sbuf("k32", [64, 512], F32)
    kb = p.sbuf("kb", [64, 512], F32)
    ki = p.sbuf("ki", [64, 512], I32)
    ang = p.sbuf("ang", [64, 512], F32)
    t1 = p.sbuf("t1", [64, T], F32)
    t2 = p.sbuf("t2", [64, T], F32)
    x2c = p.sbuf("x2c", [64, T], F32)
    ps64 = p.psum("ps64", [64, 512])
    p.dma(pos[:, :], pos_h[:, :])
    p.g("iota", ki[0:1, 0:64], [[1, 64]], channel_multiplier=0)
    p.v("tensor_copy", inv[0:1, :], ki[0:1, 0:64])
    p.act(inv[0:1, :], inv[0:1, :], AF.Exp, scale=-math.log(10000.0) / 64.0)
    for t0 in range(0, T, 512):
        p.mm(ps64[:, :], inv[0:1, :], pos[0:1, t0:t0 + 512], start=True, stop=True)
        p.v("tensor_copy", ang[:, :], ps64[:, :])
        emit_rangesin(p, sin64[:, t0:t0 + 512], ang[:, :], k32[:, :], kb[:, :], ki[:, 0:512])
        p.v("tensor_scalar_add", ang[:, :], ps64[:, :], math.pi / 2.0)
        emit_rangesin(p, cos64[:, t0:t0 + 512], ang[:, :], k32[:, :], kb[:, :], ki[:, 0:512])
    for (src_h, dst_h, sc) in ((q_h, qo_h, 1.0 / math.sqrt(128.0)), (k_h, ko_h, 1.0)):
        for c in range(FD):
            p.dma(x32[:, c * T:(c + 1) * T], src_h[c * 128:(c + 1) * 128, :])
            if sc != 1.0:
                p.act(x32[:, c * T:(c + 1) * T], x32[:, c * T:(c + 1) * T], AF.Copy, scale=sc)
        for c in range(FD):
            x1 = x32[0:64, c * T:(c + 1) * T]
            p.v("tensor_copy", x2c[:, :], x32[64:128, c * T:(c + 1) * T])
            p.v("tensor_mul", t1[:, :], x1, cos64[:, :])
            p.v("tensor_mul", t2[:, :], x2c[:, :], sin64[:, :])
            p.v("tensor_sub", t1[:, :], t1[:, :], t2[:, :])
            p.v("tensor_mul", t2[:, :], x2c[:, :], cos64[:, :])
            p.v("tensor_mul", x2c[:, :], x1, sin64[:, :])
            p.v("tensor_add", t2[:, :], t2[:, :], x2c[:, :])
            p.dma(dst_h[c * 128: c * 128 + 64, :], t1[:, :])
            p.dma(dst_h[c * 128 + 64: c * 128 + 128, :], t2[:, :])
    return p.build()


def build_l3_attn():
    S = 8192
    p = Prog()
    com = setup_common(p)
    q_h = p.hbm_in("q", [256, S])
    k_h = p.hbm_in("k", [256, S])
    v_h = p.hbm_in("v", [256, S])
    o_h = p.hbm_out("o", [256, S])
    stage = p.sbuf("stage", [128, 2048], F32)
    q32 = p.sbuf("q32", [128, S], F32)
    q16 = p.sbuf("q16", [128, S], BF16)
    k16 = p.sbuf("k16", [128, S], BF16)
    v16 = p.sbuf("v16", [128, S], BF16)
    ktm = p.sbuf("ktm", [128, 128], BF16)
    vtm = p.sbuf("vtm", [128, 128], BF16)
    kv16 = p.sbuf("kv16", [128, 128], BF16)
    ksum = p.sbuf("ksum", [128, 1], F32)
    vsum = p.sbuf("vsum", [128, 1], F32)
    r128 = p.sbuf("r128", [128, 1], F32)
    n32 = p.sbuf("n32", [128, 512], F32)
    d32 = p.sbuf("d32", [1, 512], F32)
    ps = p.psum("ps", [128, 512])
    pskv = p.psum("pskv", [128, 128])
    pst16 = p.psum("pst16", [128, 128], BF16)
    ps1 = p.psum("ps1", [1, 512])
    psb = p.psum("psb", [128, 512])
    for hl in range(2):
        r0 = hl * 128
        p.g("memset", ksum[:], 0.0)
        p.g("memset", vsum[:], 0.0)
        for s0 in range(0, S, 2048):
            p.dma(stage[:, :], q_h[r0:r0 + 128, s0:s0 + 2048])
            p.v("tensor_copy", q32[:, s0:s0 + 2048], stage[:, :])
            p.v("tensor_copy", q16[:, s0:s0 + 2048], stage[:, :])
            p.dma(stage[:, :], k_h[r0:r0 + 128, s0:s0 + 2048])
            p.v("tensor_copy", k16[:, s0:s0 + 2048], stage[:, :])
            p.v("reduce_sum", r128[:, 0:1], stage[:, :], AX.X)
            p.v("tensor_add", ksum[:, 0:1], ksum[:, 0:1], r128[:, 0:1])
            p.dma(stage[:, :], v_h[r0:r0 + 128, s0:s0 + 2048])
            p.v("tensor_copy", v16[:, s0:s0 + 2048], stage[:, :])
            p.v("reduce_sum", r128[:, 0:1], stage[:, :], AX.X)
            p.v("tensor_add", vsum[:, 0:1], vsum[:, 0:1], r128[:, 0:1])
        for j in range(64):
            p.trans(pst16[:, :], k16[:, j * 128:(j + 1) * 128], com["id16"])
            p.v("tensor_copy", ktm[:, :], pst16[:, :])
            p.trans(pst16[:, :], v16[:, j * 128:(j + 1) * 128], com["id16"])
            p.v("tensor_copy", vtm[:, :], pst16[:, :])
            p.mm(pskv[:, :], ktm[:, :], vtm[:, :], start=(j == 0), stop=(j == 63))
        p.v("tensor_copy", kv16[:, :], pskv[:, :])
        for t0 in range(0, S, 512):
            p.mm(ps[:, 0:512], kv16[:, :], q16[:, t0:t0 + 512], start=True, stop=True)
            p.act(n32[:, 0:512], ps[:, 0:512], AF.Identity, bias=vsum[:, 0:1])
            p.mm(ps1[0:1, :], ksum[:, 0:1], q32[:, t0:t0 + 512], start=True, stop=True)
            p.v("tensor_scalar_add", d32[0:1, :], ps1[0:1, :], float(S))
            p.v("reciprocal", d32[0:1, :], d32[0:1, :])
            p.mm(psb[:, 0:512], com["ones_row"], d32[0:1, :], start=True, stop=True)
            p.v("tensor_mul", n32[:, 0:512], n32[:, 0:512], psb[:, 0:512])
            p.dma(o_h[r0:r0 + 128, t0:t0 + 512], n32[:, 0:512])
    return p.build()


def build_l3_post():
    HW2 = 2052
    p = Prog()
    com = setup_common(p)
    x6_h = p.hbm_in("x6", [1024, T])
    om_h = p.hbm_in("om", [1024, T])
    v_h = p.hbm_in("v", [1024, HW2])
    msk_h = p.hbm_in("msk", [1, HW2])
    wg_h = p.hbm_in("wg", [1024, 1024])
    lw_h = p.hbm_in("lw", [5, 8, 128])
    lb_h = p.hbm_in("lb", [8, 128])
    og_h = p.hbm_out("og", [1024, T])
    stage = p.sbuf("stage", [128, HW2], F32)
    v32 = p.sbuf("v32", [128, FD * HW2], F32)
    om16 = p.sbuf("om16", [128, FD * T], BF16)
    x616 = p.sbuf("x616", [128, FD * T], BF16)
    wg16 = p.sbuf("wg16", [128, FD * 1024], BF16)
    acc = p.sbuf("acc", [128, 512], F32)
    og32 = p.sbuf("og32", [128, 512], F32)
    msk16 = p.sbuf("msk16", [1, HW2], F32)
    ps = p.psum("ps", [128, 512])
    ps2 = p.psum("ps2", [128, 512])
    ps_t = p.psum("ps_t", [128, 128])
    lw_sb = p.sbuf("lw_sb", [128, 40], F32)
    for k in range(5):
        p.dma(stage[0:8, 0:128], lw_h[k, :, :])
        p.trans(ps_t[:, 0:8], stage[0:8, 0:128], com["id32"][0:8, 0:8])
        p.v("tensor_copy", lw_sb[:, k * 8:(k + 1) * 8], ps_t[:, 0:8])
    lb_sb = col_tiles(p, "lb_sb", lb_h, stage, ps_t, com)
    load_w16(p, wg16, wg_h, 1024, 1024, stage)
    p.dma(msk16[0:1, :], msk_h[0:1, :])
    for c in range(FD):
        p.dma(stage[:, 0:HW2], v_h[c * 128:(c + 1) * 128, :])
        p.v("tensor_copy", v32[:, c * HW2:(c + 1) * HW2], stage[:, 0:HW2])
        p.dma(stage[:, 0:T], om_h[c * 128:(c + 1) * 128, :])
        p.v("tensor_copy", om16[:, c * T:(c + 1) * T], stage[:, 0:T])
        p.dma(stage[:, 0:T], x6_h[c * 128:(c + 1) * 128, :])
        p.v("tensor_copy", x616[:, c * T:(c + 1) * T], stage[:, 0:T])
    for (t0, w) in [(0, 512), (512, 512), (1024, 512), (1536, 512), (2048, 4)]:
        p.mm(ps[:, 0:w], com["ones_row"], msk16[0:1, t0:t0 + w], start=True, stop=True)
        for c in range(FD):
            p.v("tensor_mul", v32[:, c * HW2 + t0: c * HW2 + t0 + w],
                v32[:, c * HW2 + t0: c * HW2 + t0 + w], ps[:, 0:w])
    for c in range(FD):
        for t0 in range(0, T, 512):
            p.v("tensor_scalar", acc[:, 0:512], v32[:, c * HW2 + t0: c * HW2 + t0 + 512],
                lw_sb[:, c:c + 1], None, ALU.mult)
            for k in range(1, 5):
                p.v("scalar_tensor_tensor", acc[:, 0:512],
                    v32[:, c * HW2 + t0 + k: c * HW2 + t0 + k + 512],
                    lw_sb[:, k * 8 + c: k * 8 + c + 1], acc[:, 0:512], ALU.mult, ALU.add)
            p.act(acc[:, 0:512], acc[:, 0:512], AF.Identity, bias=lb_sb[:, c:c + 1])
            p.v("tensor_copy", og32[:, 0:512], om16[:, c * T + t0: c * T + t0 + 512])
            p.v("tensor_add", acc[:, 0:512], acc[:, 0:512], og32[:, 0:512])
            for kc in range(FD):
                p.mm(ps2[:, 0:512], wg16[:, kc * 1024 + c * 128: kc * 1024 + c * 128 + 128],
                     x616[:, kc * T + t0: kc * T + t0 + 512], start=(kc == 0), stop=(kc == FD - 1))
            p.v("tensor_mul", og32[:, 0:512], acc[:, 0:512], ps2[:, 0:512])
            p.dma(og_h[c * 128:(c + 1) * 128, t0:t0 + 512], og32[:, 0:512])
    return p.build()


def build_resproj_ln():
    p = Prog()
    com = setup_common(p)
    x_h = p.hbm_in("x", [1024, T])
    oin_h = p.hbm_in("oin", [1024, T])
    w_h = p.hbm_in("w", [1024, 1024])
    o_h = p.hbm_out("o", [1024, T])
    stage = p.sbuf("stage", [128, 2048], F32)
    racc = p.sbuf("racc", [128, FD * T], F32)
    oin16 = p.sbuf("oin16", [128, FD * T], BF16)
    w16 = p.sbuf("w16", [128, FD * 1024], BF16)
    sq = p.sbuf("sq", [128, 512], F32)
    r1 = p.sbuf("r1", [1, 512], F32)
    r2 = p.sbuf("r2", [1, 512], F32)
    ps = p.psum("ps", [128, 512])
    ps2 = p.psum("ps2", [128, 512])
    ps_s1 = p.psum("pss1", [1, 512])
    ps_s2 = p.psum("pss2", [1, 512])
    load_w16(p, w16, w_h, 1024, 1024, stage)
    for c in range(FD):
        p.dma(stage[:, 0:T], oin_h[c * 128:(c + 1) * 128, :])
        p.v("tensor_copy", oin16[:, c * T:(c + 1) * T], stage[:, 0:T])
    for nch in range(FD):
        p.dma(racc[:, nch * T:(nch + 1) * T], x_h[nch * 128:(nch + 1) * 128, :])
        for t0 in range(0, T, 512):
            for kc in range(FD):
                p.mm(ps[:, 0:512], w16[:, kc * 1024 + nch * 128: kc * 1024 + nch * 128 + 128],
                     oin16[:, kc * T + t0: kc * T + t0 + 512], start=(kc == 0), stop=(kc == FD - 1))
            p.v("tensor_add", racc[:, nch * T + t0: nch * T + t0 + 512],
                racc[:, nch * T + t0: nch * T + t0 + 512], ps[:, 0:512])
    emit_ln_dma(p, com, racc, o_h, T, ps, ps2, ps_s1, ps_s2, sq, r1, r2, sq)
    return p.build()


def build_ffn3a():
    p = Prog()
    com = setup_common(p)
    x_h = p.hbm_in("x", [1024, T])
    w1_h = p.hbm_in("w1", [4096, 1024])
    b1_h = p.hbm_in("b1", [32, 128])
    h_h = p.hbm_out("h", [4096, T])
    stage = p.sbuf("stage", [128, 2048], F32)
    x16 = p.sbuf("x16", [128, FD * T], BF16)
    sgn16 = p.sbuf("sgn16", [128, 1024], BF16)
    w1T = p.sbuf("w1T", [128, FD * 4096], BF16)
    alpha = p.sbuf("alpha", [128, 32], F32)
    r128 = p.sbuf("r128", [128, 1], F32)
    h32 = p.sbuf("h32", [128, 512], F32)
    ps = p.psum("ps", [128, 512])
    ps_t16 = p.psum("ps_t16", [128, 128], BF16)
    ps_t = p.psum("ps_t", [128, 128])
    for c in range(FD):
        p.dma(stage[:, 0:T], x_h[c * 128:(c + 1) * 128, :])
        p.v("tensor_copy", x16[:, c * T:(c + 1) * T], stage[:, 0:T])
    b1_sb = col_tiles(p, "b1_sb", b1_h, stage, ps_t, com)
    for pt in range(32):
        p.dma(stage[:, 0:1024], w1_h[pt * 128:(pt + 1) * 128, :])
        p.v("reduce_sum", r128[:, 0:1], stage[:, 0:1024], AX.X, apply_absolute_value=True)
        p.v("tensor_scalar_mul", alpha[:, pt:pt + 1], r128[:, 0:1], 1.0 / 1024.0)
        p.act(sgn16[:, 0:1024], stage[:, 0:1024], AF.Sign)
        for kc in range(FD):
            p.trans(ps_t16[:, :], sgn16[:, kc * 128:(kc + 1) * 128], com["id16"])
            p.v("tensor_copy", w1T[:, kc * 4096 + pt * 128: kc * 4096 + pt * 128 + 128],
                ps_t16[:, :])
    for nch in range(32):
        for t0 in range(0, T, 512):
            for kc in range(FD):
                p.mm(ps[:, 0:512], w1T[:, kc * 4096 + nch * 128: kc * 4096 + nch * 128 + 128],
                     x16[:, kc * T + t0: kc * T + t0 + 512], start=(kc == 0), stop=(kc == FD - 1))
            p.act(h32[:, 0:512], ps[:, 0:512], AF.Gelu, bias=b1_sb[:, nch:nch + 1],
                  scale=alpha[:, nch:nch + 1])
            p.dma(h_h[nch * 128:(nch + 1) * 128, t0:t0 + 512], h32[:, 0:512])
    return p.build()


def build_ffn3b():
    p = Prog()
    com = setup_common(p)
    x_h = p.hbm_in("x", [1024, T])
    h_h = p.hbm_in("h", [4096, T])
    w2_h = p.hbm_in("w2", [1024, 4096])
    b2_h = p.hbm_in("b2", [8, 128])
    o_h = p.hbm_out("o", [1024, T])
    stage = p.sbuf("stage", [128, 2048], F32)
    sgn16 = p.sbuf("sgn16", [128, 4096], BF16)
    w2T = p.sbuf("w2T", [128, 32 * 1024], BF16)
    alpha = p.sbuf("alpha", [128, 8], F32)
    r128 = p.sbuf("r128", [128, 1], F32)
    h16 = p.sbuf("h16", [128, 32 * 512], BF16)
    racc = p.sbuf("racc", [128, FD * T], F32)
    o32 = p.sbuf("o32", [128, 512], F32)
    sq = p.sbuf("sq", [128, 512], F32)
    r1 = p.sbuf("r1", [1, 512], F32)
    r2 = p.sbuf("r2", [1, 512], F32)
    ps = p.psum("ps", [128, 512])
    ps2 = p.psum("ps2", [128, 512])
    ps_t16 = p.psum("ps_t16", [128, 128], BF16)
    ps_t = p.psum("ps_t", [128, 128])
    ps_s1 = p.psum("pss1", [1, 512])
    ps_s2 = p.psum("pss2", [1, 512])
    b2_sb = col_tiles(p, "b2_sb", b2_h, stage, ps_t, com)
    for pt in range(8):
        p.g("memset", alpha[:, pt:pt + 1], 0.0)
        for hh in range(2):
            p.dma(stage[:, 0:2048], w2_h[pt * 128:(pt + 1) * 128, hh * 2048:(hh + 1) * 2048])
            p.v("reduce_sum", r128[:, 0:1], stage[:, 0:2048], AX.X, apply_absolute_value=True)
            p.v("scalar_tensor_tensor", alpha[:, pt:pt + 1], r128[:, 0:1], 1.0 / 4096.0,
                alpha[:, pt:pt + 1], ALU.mult, ALU.add)
            p.act(sgn16[:, hh * 2048:(hh + 1) * 2048], stage[:, 0:2048], AF.Sign)
        for kc in range(32):
            p.trans(ps_t16[:, :], sgn16[:, kc * 128:(kc + 1) * 128], com["id16"])
            p.v("tensor_copy", w2T[:, kc * 1024 + pt * 128: kc * 1024 + pt * 128 + 128],
                ps_t16[:, :])
    for c in range(FD):
        p.dma(racc[:, c * T:(c + 1) * T], x_h[c * 128:(c + 1) * 128, :])
    for t0 in range(0, T, 512):
        for kc in range(32):
            p.dma(stage[:, 0:512], h_h[kc * 128:(kc + 1) * 128, t0:t0 + 512])
            p.v("tensor_copy", h16[:, kc * 512:(kc + 1) * 512], stage[:, 0:512])
        for nch in range(8):
            for kc in range(32):
                p.mm(ps[:, 0:512], w2T[:, kc * 1024 + nch * 128: kc * 1024 + nch * 128 + 128],
                     h16[:, kc * 512:(kc + 1) * 512], start=(kc == 0), stop=(kc == 31))
            p.act(o32[:, 0:512], ps[:, 0:512], AF.Identity, bias=b2_sb[:, nch:nch + 1],
                  scale=alpha[:, nch:nch + 1])
            p.v("tensor_add", racc[:, nch * T + t0: nch * T + t0 + 512],
                racc[:, nch * T + t0: nch * T + t0 + 512], o32[:, 0:512])
    emit_ln_dma(p, com, racc, o_h, T, ps, ps2, ps_s1, ps_s2, sq, r1, r2, o32)
    return p.build()


_PROGS = {}


def _prog(name, fn):
    if name not in _PROGS:
        _PROGS[name] = fn()
    return _PROGS[name]


def _run(name, fn, in_maps):
    nc = _prog(name, fn)
    res = run_bass_kernel_spmd(nc, in_maps, core_ids=list(range(NC)))
    return res.results


def _fm(x):
    return np.ascontiguousarray(np.moveaxis(np.asarray(x), 2, 0).reshape(1024, 16384).astype(np.float32))


def _cat(parts):
    return np.concatenate(parts, axis=1)


def _f(a):
    return np.ascontiguousarray(np.asarray(a, dtype=np.float32))


def kernel(**inp):
    g = lambda n: _f(inp[n])
    xfm = _fm(inp["L0_attn_t"])
    sl = [slice(c * T, (c + 1) * T) for c in range(NC)]
    bs = [slice((c // 4) * 8192, (c // 4 + 1) * 8192) for c in range(NC)]

    def halo2(full, c, left, right):
        g0 = c * T
        b0 = (c // 4) * 8192
        b1 = b0 + 8192
        lo, hi = g0 - left, g0 + T + right
        out = np.zeros((full.shape[0], T + left + right), dtype=np.float32)
        src_lo, src_hi = max(lo, b0), min(hi, b1)
        out[:, src_lo - lo: src_hi - lo] = full[:, src_lo:src_hi]
        msk = np.zeros((1, T + left + right), dtype=np.float32)
        msk[0, src_lo - lo: src_hi - lo] = 1.0
        return out, msk

    w0 = _f(np.asarray(inp["L0_attn_W_qkv"]).T)
    b0 = g("L0_attn_b_qkv").reshape(24, 128)
    r = _run("proj_a", lambda: build_proj(1024, 3072), [
        {"x": xfm[:, sl[c]], "w": w0, "b": b0} for c in range(NC)])
    qkv0 = _cat([r[c]["o"] for c in range(NC)])
    r = _run("l0_attn", build_l0_attn, [
        {"q": qkv0[0:1024, sl[c]], "k": np.ascontiguousarray(qkv0[1024:2048, bs[c]]),
         "vt": np.ascontiguousarray(qkv0[2048:3072, bs[c]].T), "x": xfm[:, sl[c]]}
        for c in range(NC)])
    x1 = _cat([r[c]["o"] for c in range(NC)])
    wg0, wu0, wd0 = g("L0_ffn_w_gate"), g("L0_ffn_w_up"), g("L0_ffn_w_down")
    r = _run("ffn0a", build_ffn0a, [
        {"xtm": np.ascontiguousarray(x1[:, sl[c]].T), "wg": wg0, "wu": wu0} for c in range(NC)])
    htm = [r[c]["htm"] for c in range(NC)]
    r = _run("ffn0b", build_ffn0b, [{"htm": htm[c], "wd": wd0} for c in range(NC)])
    hq = [r[c]["hq"] for c in range(NC)]
    wdq = [r[c]["wdq"] for c in range(NC)]
    r = _run("ffn0c", build_ffn0c, [
        {"x": x1[:, sl[c]], "hq": hq[c], "wdq": wdq[c]} for c in range(NC)])
    x2 = _cat([r[c]["o"] for c in range(NC)])
    w1c = _f(np.concatenate([np.asarray(inp["L1_attn_Wq"]).T, np.asarray(inp["L1_attn_Wk"]).T,
                             np.asarray(inp["L1_attn_Wv"]).T], axis=1))
    b1c = _f(np.concatenate([inp["L1_attn_bq"], inp["L1_attn_bk"], inp["L1_attn_bv"]])).reshape(24, 128)
    r = _run("proj_a", lambda: build_proj(1024, 3072), [
        {"x": x2[:, sl[c]], "w": w1c, "b": b1c} for c in range(NC)])
    qkv1 = _cat([r[c]["o"] for c in range(NC)])
    r = _run("l1_attn", build_l1_attn, [
        {"q": np.ascontiguousarray(qkv1[256 * (c % 4):256 * (c % 4) + 256, bs[c]]),
         "k": np.ascontiguousarray(qkv1[1024 + 256 * (c % 4):1024 + 256 * (c % 4) + 256, bs[c]]),
         "v": np.ascontiguousarray(qkv1[2048 + 256 * (c % 4):2048 + 256 * (c % 4) + 256, bs[c]])}
        for c in range(NC)])
    o1 = np.zeros((1024, 16384), dtype=np.float32)
    for c in range(NC):
        o1[256 * (c % 4):256 * (c % 4) + 256, bs[c]] = r[c]["o"]
    moe_maps = []
    cw = _f(np.transpose(np.asarray(inp["L1_ffn_conv_w"]), (2, 1, 0)))
    for c in range(NC):
        x2h, _ = halo2(x2, c, 2, 0)
        oath, msk = halo2(o1, c, 2, 0)
        moe_maps.append({
            "x2": x2h, "oat": oath, "msk": msk,
            "wo": _f(np.asarray(inp["L1_attn_Wo"]).T), "bo": g("L1_attn_bo").reshape(8, 128),
            "rw": g("L1_ffn_router_w"), "rb": g("L1_ffn_router_bias").reshape(1, 16),
            "eg": g("L1_ffn_expert_gate_w"), "eu": g("L1_ffn_expert_up_w"),
            "ed": g("L1_ffn_expert_down_w"), "cw": cw,
            "su": g("L1_ffn_shared_up_w"), "sd": g("L1_ffn_shared_down_w")})
    r = _run("l1_moe", build_l1_moe, moe_maps)
    x4 = _cat([r[c]["o"] for c in range(NC)])
    perm = [h * 32 + j for h in range(16) for j in range(16)] + \
           [h * 32 + 16 + j for h in range(16) for j in range(16)]
    wqr_p = _f(np.asarray(inp["L2_attn_Wqr"])[:, perm])
    wkr_pad = np.zeros((1024, 128), dtype=np.float32)
    wkr_pad[:, 0:32] = g("L2_attn_Wkr")
    w2c = _f(np.concatenate([g("L2_attn_Wq"), wqr_p, g("L2_attn_Wkvd"), wkr_pad], axis=1))
    zb = np.zeros((15, 128), dtype=np.float32)
    r = _run("proj_b", lambda: build_proj(1024, 1920), [
        {"x": x4[:, sl[c]], "w": w2c, "b": zb} for c in range(NC)])
    p2 = _cat([r[c]["o"] for c in range(NC)])
    w2kv = _f(np.concatenate([g("L2_attn_Wku"), g("L2_attn_Wvu")], axis=1))
    zb2 = np.zeros((16, 128), dtype=np.float32)
    r = _run("proj_c", lambda: build_proj(256, 2048), [
        {"x": np.ascontiguousarray(p2[1536:1792, sl[c]]), "w": w2kv, "b": zb2} for c in range(NC)])
    kcv = _cat([r[c]["o"] for c in range(NC)])
    r = _run("rope2", build_rope2, [
        {"qr": np.ascontiguousarray(p2[1024:1536, sl[c]]),
         "kr": np.ascontiguousarray(p2[1792:1824, sl[c]]),
         "pos": (np.arange(T, dtype=np.float32) + T * (c % 4)).reshape(1, T)} for c in range(NC)])
    qro = _cat([r[c]["qo"] for c in range(NC)])
    kro = _cat([r[c]["ko"] for c in range(NC)])
    a2_maps = []
    lobo = g("L2_attn_lobo_log")
    for c in range(NC):
        b = c // 4
        h0 = (c % 4) * 4
        qh = np.zeros((384, 8192), dtype=np.float32)
        kh = np.zeros((384, 8192), dtype=np.float32)
        for hl in range(4):
            h = h0 + hl
            qh[96 * hl:96 * hl + 64] = p2[64 * h:64 * h + 64, bs[c]]
            qh[96 * hl + 64:96 * hl + 80] = qro[16 * h:16 * h + 16, bs[c]]
            qh[96 * hl + 80:96 * hl + 96] = qro[256 + 16 * h:256 + 16 * h + 16, bs[c]]
            kh[96 * hl:96 * hl + 64] = kcv[64 * h:64 * h + 64, bs[c]]
            kh[96 * hl + 64:96 * hl + 80] = kro[0:16, bs[c]]
            kh[96 * hl + 80:96 * hl + 96] = kro[16:32, bs[c]]
        a2_maps.append({"q": qh, "k": kh,
                        "v": np.ascontiguousarray(kcv[1024 + 64 * h0:1024 + 64 * h0 + 256, bs[c]]),
                        "c": lobo[h0:h0 + 4].reshape(1, 4)})
    r = _run("l2_attn", build_l2_attn, a2_maps)
    o2 = np.zeros((1024, 16384), dtype=np.float32)
    for c in range(NC):
        o2[256 * (c % 4):256 * (c % 4) + 256, bs[c]] = r[c]["o"]
    w1t = _f(np.asarray(inp["L2_ffn_w1_w"]).T)
    w2t = _f(np.asarray(inp["L2_ffn_w2_w"]).T)
    r = _run("ffn2a", build_ffn2a, [
        {"x4": x4[:, sl[c]], "oat": o2[:, sl[c]], "wo": g("L2_attn_Wo"),
         "w1": w1t, "s1": g("L2_ffn_w1_s").reshape(32, 128),
         "w2": w2t, "s2": g("L2_ffn_w2_s").reshape(32, 128)} for c in range(NC)])
    x5 = [r[c]["x5"] for c in range(NC)]
    hh = [r[c]["h"] for c in range(NC)]
    h1t = _f(np.asarray(inp["L2_ffn_h1_w"]).T.reshape(4096, 32, 128).transpose(1, 0, 2))
    r = _run("ffn2bc", build_ffn2bc, [
        {"h": hh[c], "wt": h1t, "s": g("L2_ffn_h1_s").reshape(32, 128)} for c in range(NC)])
    hh = [r[c]["o"] for c in range(NC)]
    h2t = _f(np.asarray(inp["L2_ffn_h2_w"]).T.reshape(4096, 32, 128).transpose(1, 0, 2))
    r = _run("ffn2bc", build_ffn2bc, [
        {"h": hh[c], "wt": h2t, "s": g("L2_ffn_h2_s").reshape(32, 128)} for c in range(NC)])
    hh = [r[c]["o"] for c in range(NC)]
    w3t = _f(np.asarray(inp["L2_ffn_w3_w"]).T.reshape(4096, 8, 128).transpose(1, 0, 2))
    r = _run("ffn2d", build_ffn2d, [
        {"h": hh[c], "wt": w3t, "s": g("L2_ffn_w3_s").reshape(8, 128), "x5": x5[c]}
        for c in range(NC)])
    x6 = _cat([r[c]["o"] for c in range(NC)])
    w3c = _f(np.concatenate([np.asarray(inp["L3_attn_Wq"]).T, np.asarray(inp["L3_attn_Wk"]).T,
                             np.asarray(inp["L3_attn_Wv"]).T], axis=1))
    zb3 = np.zeros((24, 128), dtype=np.float32)
    r = _run("proj_a", lambda: build_proj(1024, 3072), [
        {"x": x6[:, sl[c]], "w": w3c, "b": zb3} for c in range(NC)])
    qkv3 = _cat([r[c]["o"] for c in range(NC)])
    r = _run("rope3", build_rope3, [
        {"q": np.ascontiguousarray(qkv3[0:1024, sl[c]]),
         "k": np.ascontiguousarray(qkv3[1024:2048, sl[c]]),
         "pos": (np.arange(T, dtype=np.float32) + T * (c % 4)).reshape(1, T)} for c in range(NC)])
    qs3 = _cat([r[c]["qo"] for c in range(NC)])
    ks3 = _cat([r[c]["ko"] for c in range(NC)])
    r = _run("l3_attn", build_l3_attn, [
        {"q": np.ascontiguousarray(qs3[256 * (c % 4):256 * (c % 4) + 256, bs[c]]),
         "k": np.ascontiguousarray(ks3[256 * (c % 4):256 * (c % 4) + 256, bs[c]]),
         "v": np.ascontiguousarray(qkv3[2048 + 256 * (c % 4):2048 + 256 * (c % 4) + 256, bs[c]])}
        for c in range(NC)])
    om = np.zeros((1024, 16384), dtype=np.float32)
    for c in range(NC):
        om[256 * (c % 4):256 * (c % 4) + 256, bs[c]] = r[c]["o"]
    lw = _f(np.asarray(inp["L3_attn_lepe_w"])[:, 0, :].T.reshape(5, 8, 128))
    post_maps = []
    for c in range(NC):
        vh, msk = halo2(qkv3[2048:3072], c, 2, 2)
        post_maps.append({"x6": x6[:, sl[c]], "om": om[:, sl[c]], "v": vh, "msk": msk,
                          "wg": _f(np.asarray(inp["L3_attn_Wg"]).T), "lw": lw,
                          "lb": g("L3_attn_lepe_b").reshape(8, 128)})
    r = _run("l3_post", build_l3_post, post_maps)
    og = [r[c]["og"] for c in range(NC)]
    r = _run("resproj", build_resproj_ln, [
        {"x": x6[:, sl[c]], "oin": og[c], "w": _f(np.asarray(inp["L3_attn_Wo"]).T)}
        for c in range(NC)])
    x7 = [r[c]["o"] for c in range(NC)]
    r = _run("ffn3a", build_ffn3a, [
        {"x": x7[c], "w1": g("L3_ffn_w1"), "b1": g("L3_ffn_b1").reshape(32, 128)}
        for c in range(NC)])
    h3 = [r[c]["h"] for c in range(NC)]
    r = _run("ffn3b", build_ffn3b, [
        {"x": x7[c], "h": h3[c], "w2": g("L3_ffn_w2"), "b2": g("L3_ffn_b2").reshape(8, 128)}
        for c in range(NC)])
    out = _cat([r[c]["o"] for c in range(NC)])
    return np.ascontiguousarray(np.moveaxis(out.reshape(1024, 2, 8192), 0, 2)).astype(np.float32)
